# version 21
# baseline (speedup 1.0000x reference)
import dataclasses
import numpy as np
from contextlib import ExitStack

import concourse.bass as bass
import concourse.mybir as mybir
from concourse.bass_utils import run_bass_kernel_spmd

F32 = mybir.dt.float32
BF16 = mybir.dt.bfloat16
I16 = mybir.dt.int16
AF = mybir.ActivationFunctionType
ALU = mybir.AluOpType
AX = mybir.AxisListType

B, S, D, I = 8, 2048, 1024, 3072
F = D // 2 + 1
MAGIC = 12582912.0
QMAX = 32767.0
FCH = [(0, 103), (103, 103), (206, 103), (309, 103), (412, 101)]
NIC, NDT = I // 128, D // 128
ASCW, AMS = 256, 2
ANSC = S // ASCW
AWF = 5 * ASCW
NSC, SCW = 4, 512

SX, RX, SU, RU, SG, RG, S2, CR2, SO, RO, S2U = range(11)


def bcf(ap, n):
    return dataclasses.replace(ap, ap=list(ap.ap) + [[0, n]])


def host_tables():
    e = np.arange(D, dtype=np.float64)
    f = np.arange(F, dtype=np.float64)
    th_ef = 2.0 * np.pi * np.outer(e, f) / D
    cm = np.cos(th_ef)
    sm = -np.sin(th_ef)
    sm[:, 0] = 0.0
    sm[:, F - 1] = 0.0
    d_ = np.arange(D, dtype=np.float64)
    th_fd = 2.0 * np.pi * np.outer(f, d_) / D
    cf = np.full((F, 1), 2.0 / D)
    cf[0, 0] = 1.0 / D
    cf[F - 1, 0] = 1.0 / D
    am = cf * np.cos(th_fd)
    bm = -cf * np.sin(th_fd)
    bm[0, :] = 0.0
    bm[F - 1, :] = 0.0
    amp = np.zeros((5 * 128, D), np.float64)
    bmp = np.zeros((5 * 128, D), np.float64)
    for t, (fs, fz) in enumerate(FCH):
        amp[t * 128:t * 128 + fz] = am[fs:fs + fz]
        bmp[t * 128:t * 128 + fz] = bm[fs:fs + fz]
    ident = np.eye(128)
    return (cm.astype(np.float32), sm.astype(np.float32),
            amp.astype(np.float32), bmp.astype(np.float32),
            ident.astype(np.float32))


class KB:
    def __init__(self, nc, es):
        self.sem = {}
        self.cnt = {}
        for name in ("d", "a", "g", "T", "V", "Sc", "P", "C",
                     "L0", "L1", "L2", "L3", "X0", "X1", "A0", "A1"):
            self.sem[name] = es.enter_context(nc.semaphore("sem_" + name))
            self.cnt[name] = 0

    def inc(self, instr, s, k=1):
        instr.then_inc(self.sem[s], k)
        self.cnt[s] += k
        return self.cnt[s]

    def w(self, eng, s, n=None):
        n = self.cnt[s] if n is None else n
        assert n <= self.cnt[s], f"forward wait on {s}: {n} > {self.cnt[s]}"
        if n > 0:
            eng.wait_ge(self.sem[s], n)

    def flush(self, eng, ins, s="V"):
        self.inc(ins, s)
        eng.wait_ge(self.sem[s], self.cnt[s])


def build_program(upto="full", dbg_chunks=()):
    dbg_outs = {}
    nc = bass.Bass("TRN2", target_bir_lowering=False, debug=False, detect_race_conditions=False)

    x_in = nc.dram_tensor("x_in", [S, D], F32, kind="ExternalInput").ap()
    wq_in = nc.dram_tensor("wq_in", [D, D], F32, kind="ExternalInput").ap()
    wk_in = nc.dram_tensor("wk_in", [D, D], F32, kind="ExternalInput").ap()
    wv_in = nc.dram_tensor("wv_in", [D, D], F32, kind="ExternalInput").ap()
    wg_in = nc.dram_tensor("wg_in", [I, D], F32, kind="ExternalInput").ap()
    wu_in = nc.dram_tensor("wu_in", [I, D], F32, kind="ExternalInput").ap()
    wd_in = nc.dram_tensor("wd_in", [D, I], F32, kind="ExternalInput").ap()
    t_cm = nc.dram_tensor("t_cm", [D, F], F32, kind="ExternalInput").ap()
    t_sm = nc.dram_tensor("t_sm", [D, F], F32, kind="ExternalInput").ap()
    t_am = nc.dram_tensor("t_am", [640, D], F32, kind="ExternalInput").ap()
    t_bm = nc.dram_tensor("t_bm", [640, D], F32, kind="ExternalInput").ap()
    t_id = nc.dram_tensor("t_id", [128, 128], F32, kind="ExternalInput").ap()
    y_out = nc.dram_tensor("y_out", [S, D], F32, kind="ExternalOutput").ap()

    sp_xln = nc.dram_tensor("sp_xln", [S, D], F32).ap()
    sp_up = nc.dram_tensor("sp_up", [I, S], F32).ap()
    sp_gate = nc.dram_tensor("sp_gate", [I, S], F32).ap()
    sp_ig = nc.dram_tensor("sp_ig", [I, S], F32).ap()
    sp_i2 = nc.dram_tensor("sp_i2", [I, S], F32).ap()
    cc_i = nc.dram_tensor("cc_i", [1, 16], F32)
    cc_o = nc.dram_tensor("cc_o", [1, 16], F32)

    es = ExitStack()
    with es:
        K = KB(nc, es)
        sb = lambda n, sh, dt: es.enter_context(nc.sbuf_tensor(n, sh, dt))
        pt = lambda n, sh: es.enter_context(nc.psum_tensor(n, sh, F32))

        ident = sb("ident", [128, 128], F32)
        identb = sb("identb", [128, 128], BF16)
        ones_r = sb("ones_r", [1, 128], F32)
        cst = sb("cst", [128, 8], F32)
        scal = sb("scal", [128, 16], F32)
        scsend = sb("scsend", [1, 16], F32)
        screcv = sb("screcv", [1, 16], F32)
        bc16 = sb("bc16", [128, 16], F32)
        tmpc = sb("tmpc", [128, 4], F32)
        maxacc = sb("maxacc", [128, 224], F32)
        mred = sb("mred", [128, 4], F32)
        mrow = sb("mrow", [4, 128], F32)
        mcol = sb("mcol", [4, 1], F32)
        lns = sb("lns", [128, 8], F32)
        zbuf = sb("zbuf", [128, ASCW], F32)
        carry = sb("carry", [128, 10], F32)

        p512 = [pt(f"p{i}", [128, 512]) for i in range(4)]
        pI = pt("pI", [128, 1024])
        pI2 = pt("pI2", [128, 1024])

        block = es.enter_context(nc.Block())
        TEN = lambda f: block.tensor(f)
        VEC = lambda f: block.vector(f)
        SCA = lambda f: block.scalar(f)
        GPS = lambda f: block.gpsimd(f)
        SYN = lambda f: block.sync(f)

        def dump(tag, buf, rows=128, cols=None):
            cols = buf.shape[1] if cols is None else cols
            t = nc.dram_tensor("dbg_" + tag, [rows, cols], buf.dtype,
                               kind="ExternalOutput").ap()
            dbg_outs[tag] = t

            def _(sync, t=t, buf=buf, rows=rows, cols=cols):
                K.w(sync, "V")
                K.w(sync, "Sc")
                K.w(sync, "T")
                K.inc(sync.dma_start(out=t, in_=buf[0:rows, 0:cols]), "d", 16)
            SYN(_)

            def _(v):
                K.w(v, "d")
                K.inc(v.engine_nop(), "V")
            VEC(_)

        def _(sync):
            K.inc(sync.dma_start(out=ident[:], in_=t_id), "d", 16)
        SYN(_)

        def _(v):
            K.w(v, "d")
            K.inc(v.tensor_copy(identb[:], ident[:]), "V")
        VEC(_)

        with ExitStack() as aes:
            asb = lambda n, sh, dt: aes.enter_context(nc.sbuf_tensor(n, sh, dt))
            wfused = asb("wfused", [128, 6 * NDT * F], BF16)
            abre = asb("abre", [128, 5 * 1024], BF16)
            abim = asb("abim", [128, 5 * 1024], BF16)

            def wf_sl(tc, comp, dt, f0, fsz):
                off = ((tc * 2 + comp) * NDT + dt) * F
                return wfused[:, off + f0: off + f0 + fsz]


            with ExitStack() as pes:
                psb = lambda n, sh, dt: pes.enter_context(nc.sbuf_tensor(n, sh, dt))
                cmb = psb("cmb", [128, NDT * F], BF16)
                smb = psb("smb", [128, NDT * F], BF16)
                tfl = [psb(f"tfl{i}", [128, 1024], F32) for i in range(4)]
                wrb = [psb(f"wrb{i}", [128, NDT * D], BF16) for i in range(2)]

                tiles = ([("cm", i) for i in range(8)] + [("sm", i) for i in range(8)]
                         + [("am", i) for i in range(5)] + [("bm", i) for i in range(5)])
                tld = {}
                tcol_V = {}
                wfmm_T = {}
                wfg_T = {}
                wfev_V = {}

                def emit_tile(j, kind, i):
                    def _(sync, kind=kind, i=i, j=j):
                        if j >= 4:
                            K.w(sync, "V", tcol_V[j - 4])
                        if kind == "w":
                            tc2, et = divmod(i, NDT)
                            src = (wq_in, wk_in, wv_in)[tc2]
                            r0, wid = et * 128, 1024
                        else:
                            src = {"cm": t_cm, "sm": t_sm,
                                   "am": t_am, "bm": t_bm}[kind]
                            r0 = i * 128
                            wid = F if kind in ("cm", "sm") else 1024
                        ins = sync.dma_start(out=tfl[j % 4][:, 0:wid],
                                             in_=src[r0:r0 + 128, :])
                        K.inc(ins, f"L{j % 4}", 16)
                        tld[j] = K.cnt[f"L{j % 4}"]
                    SYN(_)

                    def _(v, kind=kind, i=i, j=j):
                        K.w(v, f"L{j % 4}", tld[j])
                        if kind == "cm":
                            dst, wid = cmb[:, i * F:(i + 1) * F], F
                        elif kind == "sm":
                            dst, wid = smb[:, i * F:(i + 1) * F], F
                        elif kind == "am":
                            dst, wid = abre[:, i * 1024:(i + 1) * 1024], 1024
                        elif kind == "bm":
                            dst, wid = abim[:, i * 1024:(i + 1) * 1024], 1024
                        else:
                            tc2, et = divmod(i, NDT)
                            if tc2 >= 2:
                                K.w(v, "T", wfmm_T[tc2 - 2])
                            dst, wid = wrb[tc2 % 2][:, et * D: et * D + 1024], 1024
                        ins = v.tensor_copy(dst, tfl[j % 4][:, 0:wid])
                        K.inc(ins, "V")
                        tcol_V[j] = K.cnt["V"]
                    VEC(_)

                def emit_build(tc, wready):
                    for gidx in range(16):
                        comp, dt = divmod(gidx, 8)
                        pa = p512[(gidx % 2) * 2]
                        pb = p512[(gidx % 2) * 2 + 1]

                        def _(t, tc=tc, comp=comp, gidx=gidx, pa=pa, pb=pb,
                              dt=dt, wready=wready):
                            if gidx == 0:
                                K.w(t, "V", wready)
                            if gidx >= 2:
                                K.w(t, "V", wfev_V[(tc, gidx - 2)])
                            elif tc >= 1:
                                K.w(t, "V", wfev_V[(tc - 1, 14 + gidx)])
                            trig = cmb if comp == 0 else smb
                            ins = None
                            for et in range(NDT):
                                c0 = et * D + dt * 128
                                t.matmul(pb[:, 0:1], wrb[tc % 2][:, c0:c0 + 128],
                                         trig[:, et * F + 512: et * F + 513],
                                         start=(et == 0), stop=(et == NDT - 1))
                                ins = t.matmul(pa[:, 0:512],
                                               wrb[tc % 2][:, c0:c0 + 128],
                                               trig[:, et * F: et * F + 512],
                                               start=(et == 0), stop=(et == NDT - 1))
                            K.inc(ins, "T")
                            wfg_T[(tc, gidx)] = K.cnt["T"]
                        TEN(_)

                        def _(v, tc=tc, comp=comp, dt=dt, pa=pa, pb=pb, gidx=gidx):
                            K.w(v, "T", wfg_T[(tc, gidx)])
                            v.tensor_copy(wf_sl(tc, comp, dt, 0, 512), pa[:, 0:512])
                            ins = v.tensor_copy(wf_sl(tc, comp, dt, 512, 1),
                                                pb[:, 0:1])
                            K.inc(ins, "V")
                            wfev_V[(tc, gidx)] = K.cnt["V"]
                        VEC(_)
                    wfmm_T[tc] = K.cnt["T"]

                for j, (kind, i) in enumerate(tiles):
                    emit_tile(j, kind, i)
                jb = len(tiles)
                wready = {}
                for tc in range(3):
                    for et in range(NDT):
                        emit_tile(jb + tc * NDT + et, "w", tc * NDT + et)
                    wready[tc] = K.cnt["V"]
                    if dbg_chunks and tc == 0:
                        dump("wrb0pre", wrb[0])
                    if tc >= 1:
                        emit_build(tc - 1, wready[tc - 1])
                        if dbg_chunks and tc == 1:
                            dump("wrb0", wrb[0])
                emit_build(2, wready[2])
            wfdone_V = K.cnt["V"]

            if dbg_chunks:
                dump("wfall", wfused)
            xin = [asb(f"xin{i}", [128, AMS * 1024], F32) for i in range(2)]
            xcb = [asb(f"xcb{i}", [128, AMS * 1024], BF16) for i in range(2)]
            xTb = [asb(f"xTb{i}", [128, NDT * ASCW], BF16) for i in range(2)]
            fb = [[asb(f"fb{t}{c}", [128, AWF], F32) for c in range(2)]
                  for t in range(3)]
            t1 = asb("t1", [128, AWF], F32)
            rec = asb("rec", [128, AWF], F32)
            gbb = [asb(f"gbb{c}", [128, AWF], BF16) for c in range(2)]
            ybuf = asb("ybuf", [128, 1024], F32)
            sqj = asb("sqj", [128, 1024], F32)
            xlnb = [asb(f"xlnb{i}", [128, 1024], F32) for i in range(2)]

            def _(v):
                K.w(v, "T")
                v.memset(ones_r[:], 1.0)
                v.memset(cst[:, 0:1], 1e-5)
                v.memset(cst[:, 1:2], 1e-8)
                v.memset(cst[:, 2:3], -1.0)
                v.memset(cst[:, 3:4], 1.0)
                v.memset(cst[:, 4:5], 2.0)
                v.memset(cst[:, 5:6], 1e-30)
                v.memset(cst[:, 6:7], 0.5)
                v.memset(cst[:, 7:8], -0.5)
                v.memset(zbuf[:], 0.0)
                v.memset(carry[:], 0.0)
                v.memset(maxacc[:], 0.0)
                v.memset(scsend[:], 0.0)
                for t in range(3):
                    for c in range(2):
                        v.memset(fb[t][c][:], 0.0)
                v.memset(t1[:], 0.0)
                v.memset(rec[:], 0.0)
                K.inc(v.engine_nop(), "V")
                K.w(v, "V")
                v.tensor_tensor_scan(out=rec[:, 0:16], data0=t1[:, 0:16],
                                     data1=zbuf[:, 0:16],
                                     initial=carry[:, 0:1],
                                     op0=ALU.add, op1=ALU.add)
                K.inc(v.engine_nop(), "V")
                K.w(v, "V")
                v.memset(rec[:, 0:16], 0.0)
                K.inc(v.engine_nop(), "V")
            VEC(_)


            att_load_d = {}
            att_cast_Sc = {}
            xtb_a = {}
            att_stt_V = {}
            dft_evac_V = {}
            dftmm_T = {}
            scan_V = {}
            gcast_V = {}
            for c in range(ANSC):
                def _(sync, c=c):
                    if c < 2:
                        K.w(sync, "T", wfmm_T[2])
                    else:
                        K.w(sync, "V", att_stt_V[(c - 2, AMS - 1)])
                    for m in range(AMS):
                        r0 = c * ASCW + m * 128
                        ins = sync.dma_start(
                            out=xin[c % 2][:, m * 1024:(m + 1) * 1024],
                            in_=x_in[r0:r0 + 128, :])
                        K.inc(ins, f"X{c % 2}", 16)
                    att_load_d[c] = K.cnt[f"X{c % 2}"]
                SYN(_)

                def _(s, c=c):
                    K.w(s, f"X{c % 2}", att_load_d[c])
                    if c >= 2:
                        K.w(s, f"A{c % 2}", xtb_a[c - 2])
                    ins = s.copy(xcb[c % 2][:], xin[c % 2][:])
                    K.inc(ins, "Sc")
                    att_cast_Sc[c] = K.cnt["Sc"]
                SCA(_)

                def _(sync, c=c):
                    K.w(sync, "Sc", att_cast_Sc[c])
                    if c >= 2:
                        K.w(sync, "T", dftmm_T[c - 2])
                    for idx in range(8):
                        dt, m = divmod(idx, AMS)
                        o0 = dt * ASCW + m * 128
                        i0 = m * 1024 + dt * 128
                        ins = sync.dma_start_transpose(
                            out=xTb[c % 2][:, o0:o0 + 128],
                            in_=xcb[c % 2][:, i0:i0 + 128])
                        K.inc(ins, f"A{c % 2}", 16)
                SYN(_)

                def _(s, c=c):
                    s.wait_ge(K.sem["Sc"], att_cast_Sc[c])
                    if c >= 2:
                        K.w(s, "T", dftmm_T[c - 2])
                    for idx in range(8, 16):
                        dt, m = divmod(idx, AMS)
                        o0 = dt * ASCW + m * 128
                        i0 = m * 1024 + dt * 128
                        ins = s.dma_start_transpose(
                            out=xTb[c % 2][:, o0:o0 + 128],
                            in_=xcb[c % 2][:, i0:i0 + 128])
                        K.inc(ins, f"A{c % 2}", 16)
                    xtb_a[c] = K.cnt[f"A{c % 2}"]
                SCA(_)

                groups = [(tc, comp, fc) for tc in range(3) for comp in range(2)
                          for fc in range(5)]
                for gi in range(0, 30, 2):
                    def _(t, c=c, gi=gi, groups=groups):
                        if gi == 0:
                            K.w(t, f"A{c % 2}", xtb_a[c])
                        if gi in (0, 2):
                            if c == 0:
                                K.w(t, "V", wfdone_V)
                            else:
                                sem, cnt2 = dft_evac_V[(c - 1, 28 if gi == 0 else 26)]
                                K.w(t, sem, cnt2)
                        elif gi >= 4:
                            sem, cnt2 = dft_evac_V[(c, gi - 4)]
                            K.w(t, sem, cnt2)
                        ins = None
                        for kk in range(2):
                            tc, comp, fc = groups[gi + kk]
                            fs, fz = FCH[fc]
                            pb = p512[(gi + kk) % 4]
                            for dt in range(NDT):
                                ins = t.matmul(pb[0:fz, 0:ASCW],
                                         wf_sl(tc, comp, dt, fs, fz),
                                         xTb[c % 2][:, dt * ASCW:(dt + 1) * ASCW],
                                         start=(dt == 0), stop=(dt == NDT - 1))
                        K.inc(ins, "T")
                    TEN(_)

                    def _(e, c=c, gi=gi, groups=groups):
                        K.w(e, "T")
                        if c > 0:
                            tcs = {groups[gi][0], groups[gi + 1][0]}
                            if tcs & {0, 2}:
                                K.w(e, "V", gcast_V[c - 1])
                            elif 1 in tcs:
                                K.w(e, "V", scan_V[c - 1])
                        ins = None
                        for kk in range(2):
                            tc, comp, fc = groups[gi + kk]
                            fs, fz = FCH[fc]
                            pb = p512[(gi + kk) % 4]
                            dst = fb[tc][comp][0:fz, fc * ASCW:(fc + 1) * ASCW]
                            srcp = pb[0:fz, 0:ASCW]
                            ins = (e.tensor_copy(dst, srcp)
                                   if hasattr(e, "tensor_copy")
                                   else e.copy(dst, srcp))
                        sem = "Sc"
                        K.inc(ins, sem)
                        dft_evac_V[(c, gi)] = (sem, K.cnt[sem])
                    SCA(_)
                dftmm_T[c] = K.cnt["T"]
                if c in dbg_chunks:
                    for tc3 in range(3):
                        for comp in range(2):
                            dump(f"spec{c}_{tc3}{comp}", fb[tc3][comp])

                for tc in range(2):
                    def _(v, tc=tc, c=c):
                        sems = {}
                        for g2 in range(tc * 10, tc * 10 + 10, 2):
                            sem, cnt2 = dft_evac_V[(c, g2)]
                            if sem != "V":
                                sems[sem] = max(sems.get(sem, 0), cnt2)
                        for sem, cnt2 in sems.items():
                            K.w(v, sem, cnt2)
                        v.tensor_tensor(out=t1[:], in0=fb[tc][0][:],
                                        in1=fb[tc][0][:], op=ALU.mult)
                        v.tensor_tensor(out=rec[:], in0=fb[tc][1][:],
                                        in1=fb[tc][1][:], op=ALU.mult)
                        ins = v.tensor_tensor(out=t1[:], in0=t1[:], in1=rec[:],
                                              op=ALU.add)
                        K.inc(ins, "V")
                    VEC(_)

                    def _(s, tc=tc):
                        K.w(s, "V")
                        s.activation(out=t1[:], in_=t1[:], func=AF.Ln,
                                     bias=cst[:, 5:6], scale=cst[:, 3:4])
                        s.activation(out=t1[:], in_=t1[:], func=AF.Exp,
                                     bias=0.0, scale=cst[:, 6:7])
                        s.activation(out=t1[:], in_=t1[:], func=AF.Ln,
                                     bias=cst[:, 1:2], scale=cst[:, 3:4])
                        ins = s.activation(out=t1[:], in_=t1[:], func=AF.Exp,
                                           bias=0.0, scale=cst[:, 2:3])
                        K.inc(ins, "Sc")
                    SCA(_)

                    def _(v, tc=tc):
                        K.w(v, "Sc")
                        v.tensor_tensor(out=fb[tc][0][:], in0=fb[tc][0][:],
                                        in1=t1[:], op=ALU.mult)
                        ins = v.tensor_tensor(out=fb[tc][1][:], in0=fb[tc][1][:],
                                              in1=t1[:], op=ALU.mult)
                        K.inc(ins, "V")
                    VEC(_)

                if c in dbg_chunks:
                    for tc3 in range(2):
                        for comp in range(2):
                            dump(f"norm{c}_{tc3}{comp}", fb[tc3][comp])

                def _(v, c=c):
                    sems = {}
                    for g2 in range(20, 30, 2):
                        sem, cnt2 = dft_evac_V[(c, g2)]
                        if sem != "V":
                            sems[sem] = max(sems.get(sem, 0), cnt2)
                    for sem, cnt2 in sems.items():
                        K.w(v, sem, cnt2)
                    a, b = fb[1][0], fb[1][1]
                    cre, cim = fb[2][0], fb[2][1]
                    v.tensor_tensor(out=t1[:], in0=a[:], in1=cre[:], op=ALU.mult)
                    v.tensor_tensor(out=rec[:], in0=a[:], in1=cim[:], op=ALU.mult)
                    v.tensor_tensor(out=a[:], in0=b[:], in1=cim[:], op=ALU.mult)
                    v.tensor_tensor(out=cim[:], in0=b[:], in1=cre[:], op=ALU.mult)
                    v.tensor_tensor(out=b[:], in0=t1[:], in1=a[:], op=ALU.subtract)
                    ins = v.tensor_tensor(out=a[:], in0=rec[:], in1=cim[:],
                                          op=ALU.add)
                    K.inc(ins, "V")
                VEC(_)
                PRRE, PRIM = fb[1][1], fb[1][0]
                if c in dbg_chunks:
                    dump(f"prodre{c}", PRRE)
                    dump(f"prodim{c}", PRIM)

                def _(v, c=c):
                    K.inc(v.engine_nop(), "V")
                    K.w(v, "V")
                    ins = None
                    for comp, srcb, dst in ((0, PRRE, fb[2][0]), (1, PRIM, fb[2][1])):
                        for fc in range(5):
                            col = comp * 5 + fc
                            ins = v.tensor_tensor_scan(
                                out=dst[:, fc * ASCW:(fc + 1) * ASCW],
                                data0=srcb[:, fc * ASCW:(fc + 1) * ASCW],
                                data1=zbuf[:],
                                initial=carry[:, col:col + 1],
                                op0=ALU.add, op1=ALU.add)
                    K.inc(ins, "V")
                    K.w(v, "V")
                    ins = None
                    for comp, dst in ((0, fb[2][0]), (1, fb[2][1])):
                        for fc in range(5):
                            col = comp * 5 + fc
                            ins = v.tensor_copy(
                                carry[:, col:col + 1],
                                dst[:, fc * ASCW + ASCW - 1: fc * ASCW + ASCW])
                    K.inc(ins, "V")
                    scan_V[c] = K.cnt["V"]
                VEC(_)
                if c in dbg_chunks:
                    dump(f"cumre{c}", fb[2][0])
                    dump(f"cumim{c}", fb[2][1])
                    dump(f"carry{c}", carry, cols=10)

                def _(v, c=c):
                    a, b = fb[2][0], fb[2][1]
                    qr, qi = fb[0][0], fb[0][1]
                    v.tensor_tensor(out=t1[:], in0=a[:], in1=qr[:], op=ALU.mult)
                    v.tensor_tensor(out=rec[:], in0=b[:], in1=qi[:], op=ALU.mult)
                    v.tensor_tensor(out=qr[:], in0=b[:], in1=qr[:], op=ALU.mult)
                    v.tensor_tensor(out=qi[:], in0=a[:], in1=qi[:], op=ALU.mult)
                    v.tensor_tensor(out=t1[:], in0=t1[:], in1=rec[:], op=ALU.add)
                    v.tensor_tensor(out=qr[:], in0=qr[:], in1=qi[:],
                                    op=ALU.subtract)
                    v.tensor_copy(gbb[0][:], t1[:])
                    ins = v.tensor_copy(gbb[1][:], qr[:])
                    K.inc(ins, "V")
                    gcast_V[c] = K.cnt["V"]
                VEC(_)
                if c in dbg_chunks:
                    dump(f"gre{c}", gbb[0])
                    dump(f"gim{c}", gbb[1])

                for m in range(AMS):
                    def _(t, c=c, m=m):
                        K.w(t, "V", gcast_V[c])
                        px = pI if m % 2 == 0 else pI2
                        ins = None
                        for bank in range(2):
                            first = True
                            for comp in range(2):
                                ab = abre if comp == 0 else abim
                                g_ = gbb[comp]
                                for fc in range(5):
                                    fs, fz = FCH[fc]
                                    ins = t.matmul(
                                        px[:, bank * 512:(bank + 1) * 512],
                                        g_[0:fz, fc * ASCW + m * 128: fc * ASCW + m * 128 + 128],
                                        ab[0:fz, fc * 1024 + bank * 512: fc * 1024 + bank * 512 + 512],
                                        start=first, stop=(comp == 1 and fc == 4))
                                    first = False
                        K.inc(ins, "T")
                    TEN(_)

                    def _(v, c=c, m=m):
                        K.w(v, "T")
                        px = pI if m % 2 == 0 else pI2
                        ins = v.scalar_tensor_tensor(
                            out=ybuf[:], in0=xin[c % 2][:, m * 1024:(m + 1) * 1024],
                            scalar=cst[:, 4:5], in1=px[:, 0:1024],
                            op0=ALU.mult, op1=ALU.add, accum_out=lns[:, 0:1])
                        K.inc(ins, "V")
                        att_stt_V[(c, m)] = K.cnt["V"]
                        v.scalar_tensor_tensor(
                            out=sqj[:], in0=ybuf[:], scalar=cst[:, 3:4],
                            in1=ybuf[:], op0=ALU.mult, op1=ALU.mult,
                            accum_out=lns[:, 1:2])
                        v.tensor_scalar(out=lns[:, 2:3], in0=lns[:, 0:1],
                                        scalar1=1.0 / D, scalar2=None, op0=ALU.mult)
                        v.tensor_scalar(out=lns[:, 3:4], in0=lns[:, 1:2],
                                        scalar1=1.0 / D, scalar2=None, op0=ALU.mult)
                        ins = v.tensor_tensor(out=lns[:, 4:5], in0=lns[:, 2:3],
                                              in1=lns[:, 2:3], op=ALU.mult)
                        K.flush(v, ins)
                        ins = v.tensor_tensor(out=lns[:, 5:6], in0=lns[:, 3:4],
                                              in1=lns[:, 4:5], op=ALU.subtract)
                        K.inc(ins, "V")
                    VEC(_)

                    if c in dbg_chunks:
                        dump(f"ybuf{c}_{m}", ybuf)

                    def _(s, m=m):
                        K.w(s, "V")
                        s.activation(out=lns[:, 6:7], in_=lns[:, 5:6],
                                     func=AF.Ln, bias=cst[:, 0:1],
                                     scale=cst[:, 3:4])
                        ins = s.activation(out=lns[:, 7:8], in_=lns[:, 6:7],
                                           func=AF.Exp, bias=0.0,
                                           scale=cst[:, 7:8])
                        K.inc(ins, "Sc")
                    SCA(_)

                    def _(v, c=c, m=m):
                        K.w(v, "Sc")
                        if (c, m) != (0, 0):
                            K.w(v, "d")
                        ins = v.tensor_scalar(out=xlnb[m % 2][:], in0=ybuf[:],
                                              scalar1=lns[:, 2:3],
                                              scalar2=lns[:, 7:8],
                                              op0=ALU.subtract, op1=ALU.mult)
                        K.flush(v, ins)
                        col = c * AMS + m
                        ins = v.tensor_reduce(out=maxacc[:, col:col + 1],
                                              in_=xlnb[m % 2][:], axis=AX.X,
                                              op=ALU.max,
                                              apply_absolute_value=True)
                        K.inc(ins, "V")
                    VEC(_)

                    def _(sync, c=c, m=m):
                        K.w(sync, "V")
                        r0 = c * ASCW + m * 128
                        ins = sync.dma_start(out=sp_xln[r0:r0 + 128, :],
                                             in_=xlnb[m % 2][:])
                        K.inc(ins, "d", 16)
                        if upto == "ln1":
                            ins = sync.dma_start(out=y_out[r0:r0 + 128, :],
                                                 in_=xlnb[m % 2][:])
                            K.inc(ins, "d", 16)
                    SYN(_)
        if upto == "ln1":
            def _(g):
                for s in ("T", "V", "Sc", "d", "a", "L0", "L1", "L2", "L3",
                          "X0", "X1", "A0", "A1"):
                    K.w(g, s)
                g.nop()
            GPS(_)
            return nc

        def emit_ar(quants, derive):
            nq = len(quants)

            def _(v):
                K.w(v, "T")
                K.w(v, "Sc")
                K.w(v, "P")
                v.engine_nop()
                K.flush(v, v.engine_nop())
                ins = None
                for j, (lo, hi) in enumerate(quants):
                    ins = v.tensor_reduce(out=mred[:, j:j + 1],
                                          in_=maxacc[:, lo:hi],
                                          axis=AX.X, op=ALU.max,
                                          apply_absolute_value=True)
                K.flush(v, ins)
                K.inc(v.engine_nop(), "V")
            VEC(_)

            def _(t):
                K.w(t, "V")
                K.inc(t.transpose(p512[0][0:nq, 0:128], mred[:, 0:nq], ident[:]),
                      "T")
            TEN(_)

            def _(v):
                K.w(v, "T")
                ins = v.tensor_copy(mrow[0:nq, :], p512[0][0:nq, 0:128])
                K.flush(v, ins)
                ins = v.tensor_reduce(out=mcol[0:nq, 0:1], in_=mrow[0:nq, :],
                                      axis=AX.X, op=ALU.max)
                K.inc(ins, "V")
            VEC(_)

            def _(t):
                K.w(t, "V")
                K.inc(t.transpose(p512[1][0:1, 0:nq], mcol[0:nq, 0:1], ident[0:nq, 0:nq]),
                      "T")
            TEN(_)

            def _(v):
                K.w(v, "T")
                K.inc(v.tensor_copy(scsend[0:1, 0:nq], p512[1][0:1, 0:nq]), "V")
            VEC(_)

            def _(g):
                for s in ("T", "V", "Sc", "d", "a"):
                    K.w(g, s)
                K.inc(g.dma_start(out=cc_i.ap(), in_=scsend[:]), "g", 16)
                K.w(g, "g")
                ins = g.collective_compute(
                    "AllReduce", ALU.max, replica_groups=[list(range(8))],
                    ins=[cc_i.ap().opt()], outs=[cc_o.ap().opt()])
                K.inc(ins, "C")
                K.w(g, "C")
                K.inc(g.dma_start(out=screcv[:], in_=cc_o.ap()), "g", 16)
                K.w(g, "g")
                K.inc(g.nop(), "P")
            GPS(_)

            def _(t):
                K.w(t, "P")
                ins = t.matmul(p512[0][:, 0:16], ones_r[0:1, 0:128],
                               screcv[0:1, 0:16], start=True, stop=True)
                K.inc(ins, "T")
            TEN(_)

            def _(v):
                K.w(v, "T")
                ins = v.tensor_copy(bc16[:], p512[0][:, 0:16])
                K.flush(v, ins)
                derive(v, lambda i: K.flush(v, i))
                ins = v.memset(maxacc[:], 0.0)
                K.inc(ins, "V")
                K.w(v, "V")
            VEC(_)

        def d0(v, fl):
            fl(v.tensor_scalar(out=scal[:, SX:SX + 1], in0=bc16[:, 0:1],
                               scalar1=1.0 / QMAX, scalar2=1e-12,
                               op0=ALU.mult, op1=ALU.max))
            fl(v.reciprocal(scal[:, RX:RX + 1], scal[:, SX:SX + 1]))
        emit_ar([(0, 16)], d0)

        with ExitStack() as pes:
            psb = lambda n, sh, dt: pes.enter_context(nc.sbuf_tensor(n, sh, dt))
            xqT = psb("xqT", [128, NDT * S], BF16)
            xlf = [psb(f"xlf{i}", [128, 1024], F32) for i in range(2)]
            xqf = psb("xqf", [128, 1024], F32)
            xqb = [psb(f"xqb{i}", [128, 1024], F32) for i in range(2)]
            wr = [psb(f"wr{i}", [128, 1024], F32) for i in range(5)]
            wqf = psb("wqf", [128, 1024], F32)
            wqf2 = psb("wqf2", [128, 1024], F32)
            wqb = [psb(f"wqb{i}", [128, 1024], F32) for i in range(3)]
            wT = [psb(f"wT{i}", [128, NDT * 128], BF16) for i in range(4)]
            wm32 = [psb(f"wm32{i}", [128, 32], F32) for i in range(3)]
            wsd = [psb(f"wsd{i}", [128, 32], F32) for i in range(3)]
            wrc = [psb(f"wrc{i}", [128, 32], F32) for i in range(3)]
            ub = [psb(f"ub{i}", [128, 512], F32) for i in range(4)]

            xq_V = {}
            xq_a = {}
            for tl in range(S // 128):
                def _(sync, tl=tl):
                    if tl >= 2:
                        K.w(sync, "V", xq_V[tl - 2])
                    ins = sync.dma_start(out=xlf[tl % 2][:],
                                         in_=sp_xln[tl * 128:(tl + 1) * 128, :])
                    K.inc(ins, "d", 16)
                SYN(_)

                def _(v, tl=tl):
                    K.w(v, "d")
                    K.w(v, "V")
                    if tl >= 2:
                        K.w(v, "V", xq_a[tl - 2])
                    v.tensor_scalar(out=xqf[:], in0=xlf[tl % 2][:],
                                    scalar1=scal[:, RX:RX + 1], scalar2=MAGIC,
                                    op0=ALU.mult, op1=ALU.add)
                    ins = v.tensor_scalar(out=xqb[tl % 2][:], in0=xqf[:],
                                          scalar1=MAGIC,
                                          scalar2=scal[:, SX:SX + 1],
                                          op0=ALU.subtract, op1=ALU.mult)
                    K.inc(ins, "V")
                    xq_V[tl] = K.cnt["V"]
                VEC(_)

                def _(t, tl=tl):
                    K.w(t, "V", xq_V[tl])
                    if tl >= 2:
                        K.w(t, "V", xq_a[tl - 2])
                    px = pI if tl % 2 == 0 else pI2
                    ins = None
                    for dt in range(NDT):
                        ins = t.transpose(px[:, dt * 128:(dt + 1) * 128],
                                          xqb[tl % 2][:, dt * 128:(dt + 1) * 128],
                                          ident[:])
                    K.inc(ins, "T")
                TEN(_)

                def _(v, tl=tl):
                    K.w(v, "T")
                    px = pI if tl % 2 == 0 else pI2
                    oap = dataclasses.replace(
                        xqT[:, tl * 128: tl * 128 + 128],
                        ap=list(xqT[:, 0:1].ap[:1]) + [[S, NDT], [1, 128]])
                    ins = v.tensor_copy(oap, px[:].rearrange("p (a b) -> p a b", b=128))
                    K.inc(ins, "V")
                    xq_a[tl] = K.cnt["V"]
                VEC(_)
            xqT_a = K.cnt["V"]

            deq_V = {}
            wt_a = {}
            wtmm_T = {}
            wtevac_Sc = {}
            mm_T = {}
            red_V = {}
            evac_Sc = {}
            spill_d = {}
            load_d = {}
            NJ = 2 * NIC

            def P_load(j):
                ic, mat = divmod(j, 2)
                wsrc = wu_in if mat == 0 else wg_in

                def _(sync, j=j, ic=ic, wsrc=wsrc):
                    if j >= 5:
                        K.w(sync, "V", deq_V[j - 5])
                    ins = sync.dma_start(out=wr[j % 5][:],
                                         in_=wsrc[ic * 128:(ic + 1) * 128, :])
                    K.inc(ins, "d", 16)
                    load_d[j] = K.cnt["d"]
                SYN(_)

            def R_ops(v, j):
                yield lambda: v.tensor_reduce(
                    out=wm32[j % 3][:].rearrange("p (b o) -> p b o", o=1),
                    in_=wr[j % 5][:].rearrange("p (b w) -> p b w", w=32),
                    axis=AX.X, op=ALU.max, apply_absolute_value=True)
                yield lambda: v.tensor_scalar(
                    out=wsd[j % 3][:], in0=wm32[j % 3][:],
                    scalar1=1.0 / 7.0, scalar2=1e-12,
                    op0=ALU.mult, op1=ALU.max)
                yield lambda: v.reciprocal(wrc[j % 3][:], wsd[j % 3][:])

            def M_ops(v, j):
                yield lambda: v.tensor_tensor(
                    out=wqf[:].rearrange("p (b w) -> p b w", w=32),
                    in0=wr[j % 5][:].rearrange("p (b w) -> p b w", w=32),
                    in1=bcf(wrc[j % 3][:], 32), op=ALU.mult)
                yield lambda: v.tensor_scalar(
                    out=wqf2[:], in0=wqf[:], scalar1=MAGIC,
                    scalar2=MAGIC, op0=ALU.add, op1=ALU.subtract)
                yield lambda: v.tensor_tensor(
                    out=wqb[j % 3][:].rearrange("p (b w) -> p b w", w=32),
                    in0=wqf2[:].rearrange("p (b w) -> p b w", w=32),
                    in1=bcf(wsd[j % 3][:], 32), op=ALU.mult)

            def P_vec(jr, jm):
                def _(v, jr=jr, jm=jm):
                    if jr is not None:
                        K.w(v, "d", load_d[jr])
                    if jm is not None and jm >= 3:
                        K.w(v, "T", wtmm_T[jm - 3])
                    rs = list(R_ops(v, jr)) if jr is not None else []
                    ms = list(M_ops(v, jm)) if jm is not None else []
                    if rs and ms:
                        for f in (rs[0], ms[0], rs[1], ms[1], rs[2]):
                            f()
                        ins = ms[2]()
                    else:
                        ms[0]()
                        ms[1]()
                        ins = ms[2]()
                    K.inc(ins, "V")
                    if jm is not None:
                        deq_V[jm] = K.cnt["V"]
                VEC(_)

            def P_T(j):
                def _(t, j=j):
                    K.w(t, "V", deq_V[j])
                    if j >= 2:
                        K.w(t, "Sc", wtevac_Sc[j - 2])
                    else:
                        K.w(t, "V", xqT_a)
                    px = pI if j % 2 == 0 else pI2
                    ins = None
                    for dt in range(NDT):
                        ins = t.transpose(px[:, dt * 128:(dt + 1) * 128],
                                          wqb[j % 3][:, dt * 128:(dt + 1) * 128],
                                          ident[:])
                    K.inc(ins, "T")
                    wtmm_T[j] = K.cnt["T"]
                TEN(_)

                def _(s, j=j):
                    K.w(s, "T", wtmm_T[j])
                    if j >= 4:
                        K.w(s, "T", mm_T[(j - 4, NSC - 1)])
                    px = pI if j % 2 == 0 else pI2
                    ins = s.copy(wT[j % 4][:], px[:])
                    K.inc(ins, "Sc")
                    wtevac_Sc[j] = K.cnt["Sc"]
                    wt_a[j] = K.cnt["Sc"]
                SCA(_)

            def emit_main(j):
                ic, mat = divmod(j, 2)
                spdst = sp_up if mat == 0 else sp_gate

                def _(t, j=j):
                    K.w(t, "V", xqT_a)
                    K.w(t, "Sc", wt_a[j])
                    for sc in range(NSC):
                        g = j * NSC + sc
                        if g >= 4:
                            K.w(t, "V", red_V[g - 4])
                        pb = p512[g % 4]
                        ins = None
                        for dt in range(NDT):
                            ins = t.matmul(
                                pb[:],
                                wT[j % 4][:, dt * 128:(dt + 1) * 128],
                                xqT[:, dt * S + sc * 512: dt * S + sc * 512 + 512],
                                start=(dt == 0), stop=(dt == NDT - 1))
                        K.inc(ins, "T")
                        mm_T[(j, sc)] = K.cnt["T"]
                TEN(_)

                def _(s, j=j):
                    for sc in range(NSC):
                        g = j * NSC + sc
                        K.w(s, "T", mm_T[(j, sc)])
                        if g >= 4:
                            K.w(s, "d", spill_d[g - 4])
                        pb = p512[g % 4]
                        ins = s.copy(ub[g % 4][:], pb[:])
                        K.inc(ins, "Sc")
                        evac_Sc[g] = K.cnt["Sc"]
                SCA(_)

                def _(v, j=j, ic=ic, mat=mat):
                    for sc in range(NSC):
                        g = j * NSC + sc
                        K.w(v, "Sc", evac_Sc[g])
                        pb = p512[g % 4]
                        col = 16 + mat * 96 + ic * NSC + sc
                        ins = v.tensor_reduce(out=maxacc[:, col:col + 1],
                                              in_=pb[:], axis=AX.X, op=ALU.max,
                                              apply_absolute_value=True)
                        K.inc(ins, "V")
                        red_V[g] = K.cnt["V"]
                VEC(_)

                def _(sync, j=j, ic=ic, spdst=spdst):
                    for sc in range(NSC):
                        g = j * NSC + sc
                        K.w(sync, "Sc", evac_Sc[g])
                        ins = sync.dma_start(
                            out=spdst[ic * 128:(ic + 1) * 128,
                                      sc * 512:(sc + 1) * 512],
                            in_=ub[g % 4][:])
                        K.inc(ins, "d", 16)
                        spill_d[g] = K.cnt["d"]
                SYN(_)

            for j in range(5):
                P_load(j)

            def _(v):
                K.w(v, "d", load_d[3])
                r0 = list(R_ops(v, 0))
                r1 = list(R_ops(v, 1))
                r2 = list(R_ops(v, 2))
                r3 = list(R_ops(v, 3))
                m0 = list(M_ops(v, 0))
                m1 = list(M_ops(v, 1))
                m2 = list(M_ops(v, 2))
                for f in (r0[0], r1[0], r0[1], r1[1], r0[2], r1[2],
                          r2[0], m0[0], r2[1], m0[1], r2[2]):
                    f()
                K.inc(m0[2](), "V")
                deq_V[0] = K.cnt["V"]
                for f in (r3[0], m1[0], r3[1], m1[1], r3[2]):
                    f()
                K.inc(m1[2](), "V")
                deq_V[1] = K.cnt["V"]
                m2[0]()
                m2[1]()
                K.inc(m2[2](), "V")
                deq_V[2] = K.cnt["V"]
            VEC(_)
            P_T(0)
            P_T(1)

            for k in range(NJ):
                if k + 5 < NJ:
                    P_load(k + 5)
                jr = k + 4 if k + 4 < NJ else None
                jm = k + 3 if k + 3 < NJ else None
                if jm is not None:
                    P_vec(jr, jm)
                if k + 2 < NJ:
                    P_T(k + 2)
                emit_main(k)

        def d1(v, fl):
            v.tensor_scalar(out=scal[:, SU:SU + 1], in0=bc16[:, 0:1],
                            scalar1=1.0 / QMAX, scalar2=1e-12,
                            op0=ALU.mult, op1=ALU.max)
            fl(v.tensor_scalar(out=scal[:, SG:SG + 1], in0=bc16[:, 1:2],
                               scalar1=1.0 / QMAX, scalar2=1e-12,
                               op0=ALU.mult, op1=ALU.max))
            v.reciprocal(scal[:, RU:RU + 1], scal[:, SU:SU + 1])
            fl(v.reciprocal(scal[:, RG:RG + 1], scal[:, SG:SG + 1]))
        emit_ar([(16, 112), (112, 208)], d1)

        with ExitStack() as pes:
            psb = lambda n, sh, dt: pes.enter_context(nc.sbuf_tensor(n, sh, dt))
            gt = [psb(f"gt{i}", [128, 2048], F32) for i in range(2)]
            igb = [psb(f"igb{i}", [128, 2048], F32) for i in range(2)]
            w1 = [psb(f"w1{i}", [128, 2048], F32) for i in range(2)]
            w2 = [psb(f"w2{i}", [128, 2048], F32) for i in range(2)]
            w3 = [psb(f"w3{i}", [128, 2048], F32) for i in range(2)]

            ld_d = {}
            q_V = {}
            sp_d = {}
            sig_Sc = {}
            ts2_V = {}
            u_P = {}
            red_V2 = {}
            def emitA2(ic):
                    def _(sync, ic=ic):
                        if ic >= 2:
                            K.w(sync, "V", q_V[ic - 2])
                        ins = sync.dma_start(out=gt[ic % 2][:],
                                             in_=sp_gate[ic * 128:(ic + 1) * 128, :])
                        K.inc(ins, "d", 16)
                        ld_d[ic] = K.cnt["d"]
                    SYN(_)

                    def _(v, ic=ic):
                        K.w(v, "d", ld_d[ic])
                        if ic == 0:
                            K.w(v, "V")
                        if ic >= 2:
                            K.w(v, "d", sp_d[ic - 2])
                            K.w(v, "P", u_P[ic - 2])
                        v.tensor_scalar(out=w1[ic % 2][:], in0=gt[ic % 2][:],
                                        scalar1=scal[:, RG:RG + 1], scalar2=MAGIC,
                                        op0=ALU.mult, op1=ALU.add)
                        ins = v.tensor_scalar(out=igb[ic % 2][:], in0=w1[ic % 2][:],
                                              scalar1=MAGIC, scalar2=None,
                                              op0=ALU.subtract)
                        K.inc(ins, "V")
                        q_V[ic] = K.cnt["V"]
                    VEC(_)

                    def _(sync, ic=ic):
                        K.w(sync, "V", q_V[ic])
                        ins = sync.dma_start(out=sp_ig[ic * 128:(ic + 1) * 128, :],
                                             in_=igb[ic % 2][:])
                        K.inc(ins, "d", 16)
                        sp_d[ic] = K.cnt["d"]
                    SYN(_)

                    def _(s, ic=ic):
                        K.w(s, "V", q_V[ic])
                        if ic >= 2:
                            K.w(s, "V", ts2_V[ic - 2])
                        ins = s.activation(out=w2[ic % 2][:], in_=igb[ic % 2][:],
                                           func=AF.Sigmoid, bias=0.0,
                                           scale=scal[:, SG:SG + 1])
                        K.inc(ins, "Sc")
                        sig_Sc[ic] = K.cnt["Sc"]
                    SCA(_)

                    def _(v, ic=ic):
                        K.w(v, "Sc", sig_Sc[ic])
                        if ic >= 2:
                            K.w(v, "P", u_P[ic - 2])
                        v.tensor_scalar(out=w3[ic % 2][:], in0=w2[ic % 2][:],
                                        scalar1=65536.0, scalar2=MAGIC,
                                        op0=ALU.mult, op1=ALU.add)
                        ins = v.tensor_scalar(out=w3[ic % 2][:], in0=w3[ic % 2][:],
                                              scalar1=MAGIC, scalar2=65535.0,
                                              op0=ALU.subtract, op1=ALU.min)
                        K.inc(ins, "V")
                        ts2_V[ic] = K.cnt["V"]
                    VEC(_)

            def emitB2(ic):
                    def _(g, ic=ic):
                        K.w(g, "V", ts2_V[ic])
                        ins = g.tensor_tensor(out=w1[ic % 2][:], in0=igb[ic % 2][:],
                                              in1=w3[ic % 2][:], op=ALU.mult)
                        K.inc(ins, "P")
                        u_P[ic] = K.cnt["P"]
                    GPS(_)

                    def _(v, ic=ic):
                        K.w(v, "P", u_P[ic])
                        ins = v.tensor_reduce(out=maxacc[:, 16 + ic:17 + ic],
                                              in_=w1[ic % 2][:], axis=AX.X,
                                              op=ALU.max, apply_absolute_value=True)
                        K.inc(ins, "V")
                        red_V2[ic] = K.cnt["V"]
                    VEC(_)

            for ic in range(NIC):
                emitA2(ic)
                if ic >= 1:
                    emitB2(ic - 1)
            emitB2(NIC - 1)
        def d2(v, fl):
            fl(v.tensor_scalar(out=tmpc[:, 0:1], in0=scal[:, SG:SG + 1],
                               scalar1=1.0 / 65536.0, scalar2=None, op0=ALU.mult))
            fl(v.tensor_tensor(out=tmpc[:, 1:2], in0=bc16[:, 0:1],
                               in1=tmpc[:, 0:1], op=ALU.mult))
            fl(v.tensor_scalar(out=scal[:, S2:S2 + 1], in0=tmpc[:, 1:2],
                               scalar1=1.0 / QMAX, scalar2=1e-12,
                               op0=ALU.mult, op1=ALU.max))
            fl(v.reciprocal(tmpc[:, 2:3], scal[:, S2:S2 + 1]))
            fl(v.tensor_tensor(out=scal[:, CR2:CR2 + 1], in0=tmpc[:, 0:1],
                               in1=tmpc[:, 2:3], op=ALU.mult))
            fl(v.tensor_tensor(out=scal[:, S2U:S2U + 1], in0=scal[:, S2:S2 + 1],
                               in1=scal[:, SU:SU + 1], op=ALU.mult))
        emit_ar([(16, 40)], d2)

        with ExitStack() as pes:
            psb = lambda n, sh, dt: pes.enter_context(nc.sbuf_tensor(n, sh, dt))
            ut = [psb(f"ut{i}", [128, 2048], F32) for i in range(2)]
            igb = [psb(f"ig3{i}", [128, 2048], F32) for i in range(2)]
            i2b = [psb(f"i2b{i}", [128, 2048], F32) for i in range(2)]
            w1 = [psb(f"w1b{i}", [128, 2048], F32) for i in range(2)]
            w2 = [psb(f"w2b{i}", [128, 2048], F32) for i in range(2)]
            w3 = [psb(f"w3b{i}", [128, 2048], F32) for i in range(2)]

            ld_d = {}
            sig_Sc = {}
            qu_V = {}
            u_P = {}
            t2_V = {}
            i2_d = {}
            o_P = {}
            red_V3 = {}
            def emitA3(ic):
                    def _(sync, ic=ic):
                        if ic >= 2:
                            K.w(sync, "V", qu_V[ic - 2])
                        ins = sync.dma_start(out=ut[ic % 2][:],
                                             in_=sp_up[ic * 128:(ic + 1) * 128, :])
                        K.inc(ins, "d", 16)
                        ins = sync.dma_start(out=igb[ic % 2][:],
                                             in_=sp_ig[ic * 128:(ic + 1) * 128, :])
                        K.inc(ins, "d", 16)
                        ld_d[ic] = K.cnt["d"]
                    SYN(_)

                    def _(s, ic=ic):
                        K.w(s, "d", ld_d[ic])
                        if ic == 0:
                            K.w(s, "V")
                        if ic >= 2:
                            K.w(s, "V", red_V3[ic - 2])
                        ins = s.activation(out=w2[ic % 2][:], in_=igb[ic % 2][:],
                                           func=AF.Sigmoid, bias=0.0,
                                           scale=scal[:, SG:SG + 1])
                        K.inc(ins, "Sc")
                        sig_Sc[ic] = K.cnt["Sc"]
                    SCA(_)

                    def _(v, ic=ic):
                        K.w(v, "Sc", sig_Sc[ic])
                        if ic == 0:
                            K.w(v, "V")
                        v.tensor_scalar(out=w1[ic % 2][:], in0=ut[ic % 2][:],
                                        scalar1=scal[:, RU:RU + 1], scalar2=MAGIC,
                                        op0=ALU.mult, op1=ALU.add)
                        v.tensor_scalar(out=w1[ic % 2][:], in0=w1[ic % 2][:],
                                        scalar1=MAGIC,
                                        scalar2=scal[:, SU:SU + 1],
                                        op0=ALU.subtract, op1=ALU.mult)
                        v.tensor_scalar(out=w2[ic % 2][:], in0=w2[ic % 2][:],
                                        scalar1=65536.0, scalar2=MAGIC,
                                        op0=ALU.mult, op1=ALU.add)
                        v.tensor_scalar(out=w2[ic % 2][:], in0=w2[ic % 2][:],
                                        scalar1=MAGIC, scalar2=65535.0,
                                        op0=ALU.subtract, op1=ALU.min)
                        ins = v.tensor_tensor(out=w3[ic % 2][:],
                                              in0=igb[ic % 2][:],
                                              in1=w2[ic % 2][:], op=ALU.mult)
                        K.inc(ins, "V")
                        qu_V[ic] = K.cnt["V"]
                    VEC(_)

            def emitB3(ic):
                    def _(v, ic=ic):
                        if ic >= 2:
                            K.w(v, "d", i2_d[ic - 2])
                        v.tensor_scalar(out=w3[ic % 2][:], in0=w3[ic % 2][:],
                                        scalar1=scal[:, CR2:CR2 + 1], scalar2=MAGIC,
                                        op0=ALU.mult, op1=ALU.add)
                        v.tensor_scalar(out=i2b[ic % 2][:], in0=w3[ic % 2][:],
                                        scalar1=MAGIC, scalar2=None,
                                        op0=ALU.subtract)
                        ins = v.tensor_scalar(out=w2[ic % 2][:], in0=w3[ic % 2][:],
                                              scalar1=MAGIC,
                                              scalar2=scal[:, S2:S2 + 1],
                                              op0=ALU.subtract, op1=ALU.mult)
                        K.inc(ins, "V")
                        t2_V[ic] = K.cnt["V"]
                        v.tensor_tensor(out=w3[ic % 2][:], in0=w1[ic % 2][:],
                                        in1=w2[ic % 2][:], op=ALU.mult)
                        ins = v.tensor_reduce(out=maxacc[:, 16 + ic:17 + ic],
                                              in_=w3[ic % 2][:], axis=AX.X,
                                              op=ALU.max, apply_absolute_value=True)
                        K.inc(ins, "V")
                        red_V3[ic] = K.cnt["V"]
                    VEC(_)

                    def _(sync, ic=ic):
                        K.w(sync, "V", t2_V[ic])
                        ins = sync.dma_start(out=sp_i2[ic * 128:(ic + 1) * 128, :],
                                             in_=i2b[ic % 2][:])
                        K.inc(ins, "d", 16)
                        i2_d[ic] = K.cnt["d"]
                    SYN(_)

            for ic in range(NIC):
                emitA3(ic)
                if ic >= 1:
                    emitB3(ic - 1)
            emitB3(NIC - 1)
        def d3(v, fl):
            fl(v.tensor_scalar(out=scal[:, SO:SO + 1], in0=bc16[:, 0:1],
                               scalar1=1.0 / QMAX, scalar2=1e-12,
                               op0=ALU.mult, op1=ALU.max))
            fl(v.reciprocal(scal[:, RO:RO + 1], scal[:, SO:SO + 1]))
        emit_ar([(16, 40)], d3)

        with ExitStack() as pes:
            psb = lambda n, sh, dt: pes.enter_context(nc.sbuf_tensor(n, sh, dt))
            wdT = psb("wdT", [128, NIC * 1024], BF16)
            oqb = [psb(f"oq{i}", [128, NIC * 512], BF16) for i in range(2)]
            wdr = [psb(f"wdr{i}", [128, 1024], F32) for i in range(3)]
            wdf = psb("wdf", [128, 1024], F32)
            wdf2 = psb("wdf2", [128, 1024], F32)
            wdb = [psb(f"wdb{i}", [128, 1024], F32) for i in range(3)]
            wm32 = psb("wm32b", [128, 32], F32)
            wsd = psb("wsdb", [128, 32], F32)
            wrc = psb("wrcb", [128, 32], F32)
            ub = [psb(f"ub4{i}", [128, 512], F32) for i in range(2)]
            i2s = [psb(f"i2s{i}", [128, 512], F32) for i in range(2)]
            v1 = [psb(f"v1{i}", [128, 512], F32) for i in range(2)]
            v2 = [psb(f"v2{i}", [128, 512], F32) for i in range(2)]
            xlf = [psb(f"xlf4{i}", [128, 1024], F32) for i in range(2)]
            ob = [psb(f"ob{i}", [128, 1024], F32) for i in range(2)]

            wld_d = {}
            wdeq_V = {}
            wdt_a = {}
            wdtmm_T = {}
            wdtevac_Sc = {}
            px_sel = lambda ic: (pI if ic % 2 == 0 else pI2)[:]
            for ic in range(NIC):
                def _(sync, ic=ic):
                    if ic >= 3:
                        K.w(sync, "V", wdeq_V[ic - 3])
                    for dt in range(NDT):
                        ins = sync.dma_start(
                            out=wdr[ic % 3][:, dt * 128:(dt + 1) * 128],
                            in_=wd_in[dt * 128:(dt + 1) * 128,
                                      ic * 128:(ic + 1) * 128])
                        K.inc(ins, "d", 16)
                    wld_d[ic] = K.cnt["d"]
                SYN(_)

                def _(v, ic=ic):
                    K.w(v, "d", wld_d[ic])
                    if ic == 0:
                        K.w(v, "V")
                    if ic >= 3:
                        K.w(v, "Sc", wdt_a[ic - 3])
                    ins = v.tensor_reduce(
                        out=wm32[:].rearrange("p (b o) -> p b o", o=1),
                        in_=wdr[ic % 3][:].rearrange("p (b w) -> p b w", w=32),
                        axis=AX.X, op=ALU.max, apply_absolute_value=True)
                    K.flush(v, ins)
                    ins = v.tensor_scalar(out=wsd[:], in0=wm32[:],
                                          scalar1=1.0 / 7.0, scalar2=1e-12,
                                          op0=ALU.mult, op1=ALU.max)
                    K.flush(v, ins)
                    ins = v.reciprocal(wrc[:], wsd[:])
                    K.flush(v, ins)
                    v.tensor_tensor(
                        out=wdf[:].rearrange("p (b w) -> p b w", w=32),
                        in0=wdr[ic % 3][:].rearrange("p (b w) -> p b w", w=32),
                        in1=bcf(wrc[:], 32), op=ALU.mult)
                    v.tensor_scalar(out=wdf2[:], in0=wdf[:], scalar1=MAGIC,
                                    scalar2=MAGIC, op0=ALU.add, op1=ALU.subtract)
                    ins = v.tensor_tensor(
                        out=wdb[ic % 3][:].rearrange("p (b w) -> p b w", w=32),
                        in0=wdf2[:].rearrange("p (b w) -> p b w", w=32),
                        in1=bcf(wsd[:], 32), op=ALU.mult)
                    K.inc(ins, "V")
                    wdeq_V[ic] = K.cnt["V"]
                VEC(_)

                def _(t, ic=ic):
                    K.w(t, "V", wdeq_V[ic])
                    if ic >= 2:
                        K.w(t, "Sc", wdtevac_Sc.get(ic - 2, 0))
                    px = pI if ic % 2 == 0 else pI2
                    ins = None
                    for dt in range(NDT):
                        ins = t.transpose(px[:, dt * 128:(dt + 1) * 128],
                                          wdb[ic % 3][:, dt * 128:(dt + 1) * 128],
                                          ident[:])
                    K.inc(ins, "T")
                    wdtmm_T[ic] = K.cnt["T"]
                TEN(_)

                def _(s, ic=ic):
                    K.w(s, "T", wdtmm_T[ic])
                    ins = s.copy(wdT[:, ic * 1024:(ic + 1) * 1024], px_sel(ic))
                    K.inc(ins, "Sc")
                    wdtevac_Sc[ic] = K.cnt["Sc"]
                    wdt_a[ic] = K.cnt["Sc"]
                SCA(_)
            wdT_all_a = K.cnt["Sc"]

            oqld_d = {}
            oq_V = {}
            oo_P = {}
            mm4_T = {}
            stt4_V = {}
            xld_d = {}
            idn_Sc = {}
            out_d = {}
            def emitA4(sc, ic):
                gg = sc * NIC + ic
                def _(sync, sc=sc, ic=ic, gg=gg):
                    if gg >= 2:
                        K.w(sync, "V", oq_V[gg - 2])
                    ins = sync.dma_start(
                        out=ub[ic % 2][:],
                        in_=sp_up[ic * 128:(ic + 1) * 128,
                                  sc * 512:(sc + 1) * 512])
                    K.inc(ins, "d", 16)
                    ins = sync.dma_start(
                        out=i2s[ic % 2][:],
                        in_=sp_i2[ic * 128:(ic + 1) * 128,
                                  sc * 512:(sc + 1) * 512])
                    K.inc(ins, "d", 16)
                    oqld_d[gg] = K.cnt["d"]
                SYN(_)

                def _(v, sc=sc, ic=ic, gg=gg):
                    K.w(v, "d", oqld_d[gg])
                    if gg == 0:
                        K.w(v, "V")
                    if sc >= 2:
                        K.w(v, "T", mm4_T[(sc - 2, 3)])
                    v.tensor_scalar(out=v1[ic % 2][:], in0=ub[ic % 2][:],
                                    scalar1=scal[:, RU:RU + 1], scalar2=MAGIC,
                                    op0=ALU.mult, op1=ALU.add)
                    v.tensor_scalar(out=v1[ic % 2][:], in0=v1[ic % 2][:],
                                    scalar1=MAGIC,
                                    scalar2=scal[:, SU:SU + 1],
                                    op0=ALU.subtract, op1=ALU.mult)
                    ins = v.tensor_scalar(out=v2[ic % 2][:], in0=i2s[ic % 2][:],
                                          scalar1=scal[:, S2:S2 + 1],
                                          scalar2=None, op0=ALU.mult)
                    K.inc(ins, "V")
                    oq_V[gg] = K.cnt["V"]
                VEC(_)

            def emitB4(sc, ic):
                gg = sc * NIC + ic
                def _(v, sc=sc, ic=ic, gg=gg):
                    v.tensor_tensor(out=v2[ic % 2][:], in0=v1[ic % 2][:],
                                    in1=v2[ic % 2][:], op=ALU.mult)
                    v.tensor_scalar(out=v2[ic % 2][:], in0=v2[ic % 2][:],
                                    scalar1=scal[:, RO:RO + 1], scalar2=MAGIC,
                                    op0=ALU.mult, op1=ALU.add)
                    ins = v.tensor_scalar(out=oqb[sc % 2][:, ic * 512:(ic + 1) * 512],
                                          in0=v2[ic % 2][:], scalar1=MAGIC,
                                          scalar2=scal[:, SO:SO + 1],
                                          op0=ALU.subtract, op1=ALU.mult)
                    K.inc(ins, "V")
                    oq_V[gg] = K.cnt["V"]
                VEC(_)

            oq_done = {}

            def emit_chains(sc):
                prev = None
                for ic in range(NIC):
                    emitA4(sc, ic)
                    if prev is not None:
                        emitB4(*prev)
                    prev = (sc, ic)
                emitB4(*prev)
                oq_done[sc] = K.cnt["V"]

            def emit_mmln(sc):
                for m in range(4):
                    def _(t, sc=sc, m=m):
                        K.w(t, "V", oq_done[sc])
                        K.w(t, "Sc", wdT_all_a)
                        if (sc * 4 + m) >= 2:
                            pm = sc * 4 + m - 2
                            K.w(t, "V", stt4_V[pm])
                        px = pI if m % 2 == 0 else pI2
                        ins = None
                        for bank in range(2):
                            for ic2 in range(NIC):
                                ins = t.matmul(
                                    px[:, bank * 512:(bank + 1) * 512],
                                    oqb[sc % 2][:, ic2 * 512 + m * 128: ic2 * 512 + m * 128 + 128],
                                    wdT[:, ic2 * 1024 + bank * 512: ic2 * 1024 + bank * 512 + 512],
                                    start=(ic2 == 0), stop=(ic2 == NIC - 1))
                        K.inc(ins, "T")
                        mm4_T[(sc, m)] = K.cnt["T"]
                    TEN(_)

                    def _(sync, sc=sc, m=m):
                        sm = sc * 4 + m
                        if sm >= 2:
                            K.w(sync, "V", stt4_V[sm - 2])
                        r0 = sc * 512 + m * 128
                        ins = sync.dma_start(out=xlf[m % 2][:],
                                             in_=sp_xln[r0:r0 + 128, :])
                        K.inc(ins, "d", 16)
                        xld_d[sm] = K.cnt["d"]
                    SYN(_)

                    def _(v, sc=sc, m=m):
                        sm = sc * 4 + m
                        K.w(v, "T", mm4_T[(sc, m)])
                        K.w(v, "d", xld_d[sm])
                        if sm >= 2:
                            K.w(v, "d", out_d[sm - 2])
                        px = pI if m % 2 == 0 else pI2
                        ins = v.scalar_tensor_tensor(
                            out=ob[m % 2][:], in0=px[:, 0:1024],
                            scalar=cst[:, 3:4], in1=xlf[m % 2][:],
                            op0=ALU.mult, op1=ALU.add, accum_out=lns[:, 0:1])
                        K.inc(ins, "V")
                        stt4_V[sm] = K.cnt["V"]
                        v.scalar_tensor_tensor(
                            out=wdf[:], in0=ob[m % 2][:], scalar=cst[:, 3:4],
                            in1=ob[m % 2][:], op0=ALU.mult, op1=ALU.mult,
                            accum_out=lns[:, 1:2])
                        v.tensor_scalar(out=lns[:, 2:3], in0=lns[:, 0:1],
                                        scalar1=1.0 / D, scalar2=None, op0=ALU.mult)
                        v.tensor_scalar(out=lns[:, 3:4], in0=lns[:, 1:2],
                                        scalar1=1.0 / D, scalar2=None, op0=ALU.mult)
                        ins = v.tensor_tensor(out=lns[:, 4:5], in0=lns[:, 2:3],
                                              in1=lns[:, 2:3], op=ALU.mult)
                        K.flush(v, ins)
                        ins = v.tensor_tensor(out=lns[:, 5:6], in0=lns[:, 3:4],
                                              in1=lns[:, 4:5], op=ALU.subtract)
                        K.inc(ins, "V")
                    VEC(_)

                    if c in dbg_chunks:
                        dump(f"ybuf{c}_{m}", ybuf)

                    def _(s, m=m):
                        K.w(s, "V")
                        s.activation(out=lns[:, 6:7], in_=lns[:, 5:6],
                                     func=AF.Ln, bias=cst[:, 0:1],
                                     scale=cst[:, 3:4])
                        ins = s.activation(out=lns[:, 7:8], in_=lns[:, 6:7],
                                           func=AF.Exp, bias=0.0,
                                           scale=cst[:, 7:8])
                        K.inc(ins, "Sc")
                    SCA(_)

                    def _(v, sc=sc, m=m):
                        sm = sc * 4 + m
                        K.w(v, "Sc")
                        ins = v.tensor_scalar(out=ob[m % 2][:], in0=ob[m % 2][:],
                                              scalar1=lns[:, 2:3],
                                              scalar2=lns[:, 7:8],
                                              op0=ALU.subtract, op1=ALU.mult)
                        K.inc(ins, "V")
                        idn_Sc[sm] = ("V", K.cnt["V"])
                    VEC(_)

                    def _(sync, sc=sc, m=m):
                        sm = sc * 4 + m
                        sem, cnt2 = idn_Sc[sm]
                        K.w(sync, sem, cnt2)
                        r0 = sc * 512 + m * 128
                        ins = sync.dma_start(out=y_out[r0:r0 + 128, :],
                                             in_=ob[m % 2][:])
                        K.inc(ins, "d", 16)
                        out_d[sm] = K.cnt["d"]
                    SYN(_)

            for sc in range(NSC):
                emit_chains(sc)
                if sc >= 1:
                    emit_mmln(sc - 1)
            emit_mmln(NSC - 1)

        def _(g):
            for s in ("T", "V", "Sc", "d", "a", "g", "L0", "L1", "L2", "L3",
                      "X0", "X1", "A0", "A1"):
                K.w(g, s)
            g.nop()
        GPS(_)

    return nc


_TABLES = None


def kernel(L0_attn_x, L0_attn_Wq, L0_attn_Wk, L0_attn_Wv,
           L0_ffn_w_gate, L0_ffn_w_up, L0_ffn_w_down):
    global _TABLES
    if _TABLES is None:
        _TABLES = host_tables()
    cm, sm, am, bm, ident = _TABLES
    x = np.ascontiguousarray(np.asarray(L0_attn_x, dtype=np.float32))
    wq = np.ascontiguousarray(np.asarray(L0_attn_Wq, dtype=np.float32))
    wk = np.ascontiguousarray(np.asarray(L0_attn_Wk, dtype=np.float32))
    wv = np.ascontiguousarray(np.asarray(L0_attn_Wv, dtype=np.float32))
    wg = np.ascontiguousarray(np.asarray(L0_ffn_w_gate, dtype=np.float32))
    wu = np.ascontiguousarray(np.asarray(L0_ffn_w_up, dtype=np.float32))
    wd = np.ascontiguousarray(np.asarray(L0_ffn_w_down, dtype=np.float32))

    nc = build_program()
    in_maps = []
    for c in range(8):
        in_maps.append({
            "x_in": x[c], "wq_in": wq, "wk_in": wk, "wv_in": wv,
            "wg_in": wg, "wu_in": wu, "wd_in": wd,
            "t_cm": cm, "t_sm": sm, "t_am": am, "t_bm": bm, "t_id": ident,
        })
    res = run_bass_kernel_spmd(nc, in_maps, core_ids=list(range(8)))
    out = np.stack([res.results[c]["y_out"] for c in range(8)], axis=0)
    return np.ascontiguousarray(out.astype(np.float32))


# version 22 (final)
# speedup vs baseline: 1.1105x; 1.1105x over previous
import dataclasses
import numpy as np
from contextlib import ExitStack

import concourse.bass as bass
import concourse.mybir as mybir
from concourse.bass_utils import run_bass_kernel_spmd

F32 = mybir.dt.float32
BF16 = mybir.dt.bfloat16
I16 = mybir.dt.int16
AF = mybir.ActivationFunctionType
ALU = mybir.AluOpType
AX = mybir.AxisListType

B, S, D, I = 8, 2048, 1024, 3072
F = D // 2 + 1
MAGIC = 12582912.0
QMAX = 32767.0
FCH = [(0, 103), (103, 103), (206, 103), (309, 103), (412, 101)]
NIC, NDT = I // 128, D // 128
ASCW, AMS = 256, 2
ANSC = S // ASCW
AWF = 5 * ASCW
NSC, SCW = 4, 512

SX, RX, SU, RU, SG, RG, S2, CR2, SO, RO, S2U = range(11)


def bcf(ap, n):
    return dataclasses.replace(ap, ap=list(ap.ap) + [[0, n]])


def host_tables():
    e = np.arange(D, dtype=np.float64)
    f = np.arange(F, dtype=np.float64)
    th_ef = 2.0 * np.pi * np.outer(e, f) / D
    cm = np.cos(th_ef)
    sm = -np.sin(th_ef)
    sm[:, 0] = 0.0
    sm[:, F - 1] = 0.0
    d_ = np.arange(D, dtype=np.float64)
    th_fd = 2.0 * np.pi * np.outer(f, d_) / D
    cf = np.full((F, 1), 2.0 / D)
    cf[0, 0] = 1.0 / D
    cf[F - 1, 0] = 1.0 / D
    am = cf * np.cos(th_fd)
    bm = -cf * np.sin(th_fd)
    bm[0, :] = 0.0
    bm[F - 1, :] = 0.0
    amp = np.zeros((5 * 128, D), np.float64)
    bmp = np.zeros((5 * 128, D), np.float64)
    for t, (fs, fz) in enumerate(FCH):
        amp[t * 128:t * 128 + fz] = am[fs:fs + fz]
        bmp[t * 128:t * 128 + fz] = bm[fs:fs + fz]
    ident = np.eye(128)
    return (cm.astype(np.float32), sm.astype(np.float32),
            amp.astype(np.float32), bmp.astype(np.float32),
            ident.astype(np.float32))


class KB:
    def __init__(self, nc, es):
        self.sem = {}
        self.cnt = {}
        for name in ("d", "a", "g", "T", "V", "Sc", "P", "C",
                     "L0", "L1", "L2", "L3", "X0", "X1", "A0", "A1"):
            self.sem[name] = es.enter_context(nc.semaphore("sem_" + name))
            self.cnt[name] = 0

    def inc(self, instr, s, k=1):
        instr.then_inc(self.sem[s], k)
        self.cnt[s] += k
        return self.cnt[s]

    def w(self, eng, s, n=None):
        n = self.cnt[s] if n is None else n
        assert n <= self.cnt[s], f"forward wait on {s}: {n} > {self.cnt[s]}"
        if n > 0:
            eng.wait_ge(self.sem[s], n)

    def flush(self, eng, ins, s="V"):
        self.inc(ins, s)
        eng.wait_ge(self.sem[s], self.cnt[s])


def build_program(upto="full", dbg_chunks=()):
    dbg_outs = {}
    nc = bass.Bass("TRN2", target_bir_lowering=False, debug=False, detect_race_conditions=False)

    x_in = nc.dram_tensor("x_in", [S, D], F32, kind="ExternalInput").ap()
    wq_in = nc.dram_tensor("wq_in", [D, D], F32, kind="ExternalInput").ap()
    wk_in = nc.dram_tensor("wk_in", [D, D], F32, kind="ExternalInput").ap()
    wv_in = nc.dram_tensor("wv_in", [D, D], F32, kind="ExternalInput").ap()
    wg_in = nc.dram_tensor("wg_in", [I, D], F32, kind="ExternalInput").ap()
    wu_in = nc.dram_tensor("wu_in", [I, D], F32, kind="ExternalInput").ap()
    wd_in = nc.dram_tensor("wd_in", [D, I], F32, kind="ExternalInput").ap()
    t_cm = nc.dram_tensor("t_cm", [D, F], F32, kind="ExternalInput").ap()
    t_sm = nc.dram_tensor("t_sm", [D, F], F32, kind="ExternalInput").ap()
    t_am = nc.dram_tensor("t_am", [640, D], F32, kind="ExternalInput").ap()
    t_bm = nc.dram_tensor("t_bm", [640, D], F32, kind="ExternalInput").ap()
    t_id = nc.dram_tensor("t_id", [128, 128], F32, kind="ExternalInput").ap()
    y_out = nc.dram_tensor("y_out", [S, D], F32, kind="ExternalOutput").ap()

    sp_xln = nc.dram_tensor("sp_xln", [S, D], F32).ap()
    sp_up = nc.dram_tensor("sp_up", [I, S], F32).ap()
    sp_gate = nc.dram_tensor("sp_gate", [I, S], F32).ap()
    sp_ig = nc.dram_tensor("sp_ig", [I, S], F32).ap()
    sp_i2 = nc.dram_tensor("sp_i2", [I, S], F32).ap()
    cc_i = nc.dram_tensor("cc_i", [1, 16], F32)
    cc_o = nc.dram_tensor("cc_o", [1, 16], F32)

    es = ExitStack()
    with es:
        K = KB(nc, es)
        sb = lambda n, sh, dt: es.enter_context(nc.sbuf_tensor(n, sh, dt))
        pt = lambda n, sh: es.enter_context(nc.psum_tensor(n, sh, F32))

        ident = sb("ident", [128, 128], F32)
        identb = sb("identb", [128, 128], BF16)
        ones_r = sb("ones_r", [1, 128], F32)
        cst = sb("cst", [128, 8], F32)
        scal = sb("scal", [128, 16], F32)
        scsend = sb("scsend", [1, 16], F32)
        screcv = sb("screcv", [1, 16], F32)
        bc16 = sb("bc16", [128, 16], F32)
        tmpc = sb("tmpc", [128, 4], F32)
        maxacc = sb("maxacc", [128, 224], F32)
        mred = sb("mred", [128, 4], F32)
        mrow = sb("mrow", [4, 128], F32)
        mcol = sb("mcol", [4, 1], F32)
        lns = sb("lns", [128, 8], F32)
        zbuf = sb("zbuf", [128, ASCW], F32)
        carry = sb("carry", [128, 10], F32)

        p512 = [pt(f"p{i}", [128, 512]) for i in range(4)]
        pI = pt("pI", [128, 1024])
        pI2 = pt("pI2", [128, 1024])

        block = es.enter_context(nc.Block())
        TEN = lambda f: block.tensor(f)
        VEC = lambda f: block.vector(f)
        SCA = lambda f: block.scalar(f)
        GPS = lambda f: block.gpsimd(f)
        SYN = lambda f: block.sync(f)

        def dump(tag, buf, rows=128, cols=None):
            cols = buf.shape[1] if cols is None else cols
            t = nc.dram_tensor("dbg_" + tag, [rows, cols], buf.dtype,
                               kind="ExternalOutput").ap()
            dbg_outs[tag] = t

            def _(sync, t=t, buf=buf, rows=rows, cols=cols):
                K.w(sync, "V")
                K.w(sync, "Sc")
                K.w(sync, "T")
                K.inc(sync.dma_start(out=t, in_=buf[0:rows, 0:cols]), "d", 16)
            SYN(_)

            def _(v):
                K.w(v, "d")
                K.inc(v.engine_nop(), "V")
            VEC(_)

        def _(sync):
            K.inc(sync.dma_start(out=ident[:], in_=t_id), "d", 16)
        SYN(_)

        def _(v):
            K.w(v, "d")
            K.inc(v.tensor_copy(identb[:], ident[:]), "V")
        VEC(_)

        with ExitStack() as aes:
            asb = lambda n, sh, dt: aes.enter_context(nc.sbuf_tensor(n, sh, dt))
            wfused = asb("wfused", [128, 6 * NDT * F], BF16)
            abre = asb("abre", [128, 5 * 1024], BF16)
            abim = asb("abim", [128, 5 * 1024], BF16)

            def wf_sl(tc, comp, dt, f0, fsz):
                off = ((tc * 2 + comp) * NDT + dt) * F
                return wfused[:, off + f0: off + f0 + fsz]


            with ExitStack() as pes:
                psb = lambda n, sh, dt: pes.enter_context(nc.sbuf_tensor(n, sh, dt))
                cmb = psb("cmb", [128, NDT * F], BF16)
                smb = psb("smb", [128, NDT * F], BF16)
                tfl = [psb(f"tfl{i}", [128, 1024], F32) for i in range(4)]
                wrb = [psb(f"wrb{i}", [128, NDT * D], BF16) for i in range(2)]

                tiles = ([("cm", i) for i in range(8)] + [("sm", i) for i in range(8)]
                         + [("am", i) for i in range(5)] + [("bm", i) for i in range(5)])
                tld = {}
                tcol_V = {}
                wfmm_T = {}
                wfg_T = {}
                wfev_V = {}

                def emit_tile(j, kind, i):
                    def _(sync, kind=kind, i=i, j=j):
                        if j >= 4:
                            K.w(sync, "V", tcol_V[j - 4])
                        if kind == "w":
                            tc2, et = divmod(i, NDT)
                            src = (wq_in, wk_in, wv_in)[tc2]
                            r0, wid = et * 128, 1024
                        else:
                            src = {"cm": t_cm, "sm": t_sm,
                                   "am": t_am, "bm": t_bm}[kind]
                            r0 = i * 128
                            wid = F if kind in ("cm", "sm") else 1024
                        ins = sync.dma_start(out=tfl[j % 4][:, 0:wid],
                                             in_=src[r0:r0 + 128, :])
                        K.inc(ins, f"L{j % 4}", 16)
                        tld[j] = K.cnt[f"L{j % 4}"]
                    SYN(_)

                    def _(v, kind=kind, i=i, j=j):
                        K.w(v, f"L{j % 4}", tld[j])
                        if kind == "cm":
                            dst, wid = cmb[:, i * F:(i + 1) * F], F
                        elif kind == "sm":
                            dst, wid = smb[:, i * F:(i + 1) * F], F
                        elif kind == "am":
                            dst, wid = abre[:, i * 1024:(i + 1) * 1024], 1024
                        elif kind == "bm":
                            dst, wid = abim[:, i * 1024:(i + 1) * 1024], 1024
                        else:
                            tc2, et = divmod(i, NDT)
                            if tc2 >= 2:
                                K.w(v, "T", wfmm_T[tc2 - 2])
                            dst, wid = wrb[tc2 % 2][:, et * D: et * D + 1024], 1024
                        ins = v.tensor_copy(dst, tfl[j % 4][:, 0:wid])
                        K.inc(ins, "V")
                        tcol_V[j] = K.cnt["V"]
                    VEC(_)

                def emit_build(tc, wready):
                    for gidx in range(16):
                        comp, dt = divmod(gidx, 8)
                        pa = p512[(gidx % 2) * 2]
                        pb = p512[(gidx % 2) * 2 + 1]

                        def _(t, tc=tc, comp=comp, gidx=gidx, pa=pa, pb=pb,
                              dt=dt, wready=wready):
                            if gidx == 0:
                                K.w(t, "V", wready)
                            if gidx >= 2:
                                K.w(t, "V", wfev_V[(tc, gidx - 2)])
                            elif tc >= 1:
                                K.w(t, "V", wfev_V[(tc - 1, 14 + gidx)])
                            trig = cmb if comp == 0 else smb
                            ins = None
                            for et in range(NDT):
                                c0 = et * D + dt * 128
                                t.matmul(pb[:, 0:1], wrb[tc % 2][:, c0:c0 + 128],
                                         trig[:, et * F + 512: et * F + 513],
                                         start=(et == 0), stop=(et == NDT - 1))
                                ins = t.matmul(pa[:, 0:512],
                                               wrb[tc % 2][:, c0:c0 + 128],
                                               trig[:, et * F: et * F + 512],
                                               start=(et == 0), stop=(et == NDT - 1))
                            K.inc(ins, "T")
                            wfg_T[(tc, gidx)] = K.cnt["T"]
                        TEN(_)

                        def _(v, tc=tc, comp=comp, dt=dt, pa=pa, pb=pb, gidx=gidx):
                            K.w(v, "T", wfg_T[(tc, gidx)])
                            v.tensor_copy(wf_sl(tc, comp, dt, 0, 512), pa[:, 0:512])
                            ins = v.tensor_copy(wf_sl(tc, comp, dt, 512, 1),
                                                pb[:, 0:1])
                            K.inc(ins, "V")
                            wfev_V[(tc, gidx)] = K.cnt["V"]
                        VEC(_)
                    wfmm_T[tc] = K.cnt["T"]

                for j, (kind, i) in enumerate(tiles):
                    emit_tile(j, kind, i)
                jb = len(tiles)
                wready = {}
                for tc in range(3):
                    for et in range(NDT):
                        emit_tile(jb + tc * NDT + et, "w", tc * NDT + et)
                    wready[tc] = K.cnt["V"]
                    if dbg_chunks and tc == 0:
                        dump("wrb0pre", wrb[0])
                    if tc >= 1:
                        emit_build(tc - 1, wready[tc - 1])
                        if dbg_chunks and tc == 1:
                            dump("wrb0", wrb[0])
                emit_build(2, wready[2])
            wfdone_V = K.cnt["V"]

            if dbg_chunks:
                dump("wfall", wfused)
            xin = [asb(f"xin{i}", [128, AMS * 1024], F32) for i in range(2)]
            xcb = [asb(f"xcb{i}", [128, AMS * 1024], BF16) for i in range(2)]
            xTb = [asb(f"xTb{i}", [128, NDT * ASCW], BF16) for i in range(2)]
            fb = [[asb(f"fb{t}{c}", [128, AWF], F32) for c in range(2)]
                  for t in range(3)]
            t1 = asb("t1", [128, AWF], F32)
            rec = asb("rec", [128, AWF], F32)
            gbb = [asb(f"gbb{c}", [128, AWF], BF16) for c in range(2)]
            ybuf = asb("ybuf", [128, 1024], F32)
            sqj = asb("sqj", [128, 1024], F32)
            xlnb = [asb(f"xlnb{i}", [128, 1024], F32) for i in range(2)]

            def _(v):
                K.w(v, "T")
                v.memset(ones_r[:], 1.0)
                v.memset(cst[:, 0:1], 1e-5)
                v.memset(cst[:, 1:2], 1e-8)
                v.memset(cst[:, 2:3], -1.0)
                v.memset(cst[:, 3:4], 1.0)
                v.memset(cst[:, 4:5], 2.0)
                v.memset(cst[:, 5:6], 1e-30)
                v.memset(cst[:, 6:7], 0.5)
                v.memset(cst[:, 7:8], -0.5)
                v.memset(zbuf[:], 0.0)
                v.memset(carry[:], 0.0)
                v.memset(maxacc[:], 0.0)
                v.memset(scsend[:], 0.0)
                for t in range(3):
                    for c in range(2):
                        v.memset(fb[t][c][:], 0.0)
                v.memset(t1[:], 0.0)
                v.memset(rec[:], 0.0)
                K.inc(v.engine_nop(), "V")
                K.w(v, "V")
                v.tensor_tensor_scan(out=rec[:, 0:16], data0=t1[:, 0:16],
                                     data1=zbuf[:, 0:16],
                                     initial=carry[:, 0:1],
                                     op0=ALU.add, op1=ALU.add)
                K.inc(v.engine_nop(), "V")
                K.w(v, "V")
                v.memset(rec[:, 0:16], 0.0)
                K.inc(v.engine_nop(), "V")
            VEC(_)


            att_load_d = {}
            att_cast_Sc = {}
            xtb_a = {}
            att_stt_V = {}
            dft_evac_V = {}
            dftmm_T = {}
            scan_V = {}
            gcast_V = {}
            for c in range(ANSC):
                def _(sync, c=c):
                    if c < 2:
                        K.w(sync, "T", wfmm_T[2])
                    else:
                        K.w(sync, "V", att_stt_V[(c - 2, AMS - 1)])
                    for m in range(AMS):
                        r0 = c * ASCW + m * 128
                        ins = sync.dma_start(
                            out=xin[c % 2][:, m * 1024:(m + 1) * 1024],
                            in_=x_in[r0:r0 + 128, :])
                        K.inc(ins, f"X{c % 2}", 16)
                    att_load_d[c] = K.cnt[f"X{c % 2}"]
                SYN(_)

                def _(s, c=c):
                    K.w(s, f"X{c % 2}", att_load_d[c])
                    if c >= 2:
                        K.w(s, f"A{c % 2}", xtb_a[c - 2])
                    ins = s.copy(xcb[c % 2][:], xin[c % 2][:])
                    K.inc(ins, "Sc")
                    att_cast_Sc[c] = K.cnt["Sc"]
                SCA(_)

                def _(sync, c=c):
                    K.w(sync, "Sc", att_cast_Sc[c])
                    if c >= 2:
                        K.w(sync, "T", dftmm_T[c - 2])
                    for idx in range(8):
                        dt, m = divmod(idx, AMS)
                        o0 = dt * ASCW + m * 128
                        i0 = m * 1024 + dt * 128
                        ins = sync.dma_start_transpose(
                            out=xTb[c % 2][:, o0:o0 + 128],
                            in_=xcb[c % 2][:, i0:i0 + 128])
                        K.inc(ins, f"A{c % 2}", 16)
                SYN(_)

                def _(s, c=c):
                    s.wait_ge(K.sem["Sc"], att_cast_Sc[c])
                    if c >= 2:
                        K.w(s, "T", dftmm_T[c - 2])
                    for idx in range(8, 16):
                        dt, m = divmod(idx, AMS)
                        o0 = dt * ASCW + m * 128
                        i0 = m * 1024 + dt * 128
                        ins = s.dma_start_transpose(
                            out=xTb[c % 2][:, o0:o0 + 128],
                            in_=xcb[c % 2][:, i0:i0 + 128])
                        K.inc(ins, f"A{c % 2}", 16)
                    xtb_a[c] = K.cnt[f"A{c % 2}"]
                SCA(_)

                groups = [(tc, comp, fc) for tc in range(3) for comp in range(2)
                          for fc in range(5)]
                for gi in range(0, 30, 2):
                    def _(t, c=c, gi=gi, groups=groups):
                        if gi == 0:
                            K.w(t, f"A{c % 2}", xtb_a[c])
                        if gi in (0, 2):
                            if c == 0:
                                K.w(t, "V", wfdone_V)
                            else:
                                sem, cnt2 = dft_evac_V[(c - 1, 28 if gi == 0 else 26)]
                                K.w(t, sem, cnt2)
                        elif gi >= 4:
                            sem, cnt2 = dft_evac_V[(c, gi - 4)]
                            K.w(t, sem, cnt2)
                        ins = None
                        for kk in range(2):
                            tc, comp, fc = groups[gi + kk]
                            fs, fz = FCH[fc]
                            pb = p512[(gi + kk) % 4]
                            for dt in range(NDT):
                                ins = t.matmul(pb[0:fz, 0:ASCW],
                                         wf_sl(tc, comp, dt, fs, fz),
                                         xTb[c % 2][:, dt * ASCW:(dt + 1) * ASCW],
                                         start=(dt == 0), stop=(dt == NDT - 1))
                        K.inc(ins, "T")
                    TEN(_)

                    def _(e, c=c, gi=gi, groups=groups):
                        K.w(e, "T")
                        if c > 0:
                            tcs = {groups[gi][0], groups[gi + 1][0]}
                            if tcs & {0, 2}:
                                K.w(e, "V", gcast_V[c - 1])
                            elif 1 in tcs:
                                K.w(e, "V", scan_V[c - 1])
                        ins = None
                        for kk in range(2):
                            tc, comp, fc = groups[gi + kk]
                            fs, fz = FCH[fc]
                            pb = p512[(gi + kk) % 4]
                            dst = fb[tc][comp][0:fz, fc * ASCW:(fc + 1) * ASCW]
                            srcp = pb[0:fz, 0:ASCW]
                            ins = (e.tensor_copy(dst, srcp)
                                   if hasattr(e, "tensor_copy")
                                   else e.copy(dst, srcp))
                        sem = "Sc"
                        K.inc(ins, sem)
                        dft_evac_V[(c, gi)] = (sem, K.cnt[sem])
                    SCA(_)
                dftmm_T[c] = K.cnt["T"]
                if c in dbg_chunks:
                    for tc3 in range(3):
                        for comp in range(2):
                            dump(f"spec{c}_{tc3}{comp}", fb[tc3][comp])

                for tc in range(2):
                    def _(v, tc=tc, c=c):
                        sems = {}
                        for g2 in range(tc * 10, tc * 10 + 10, 2):
                            sem, cnt2 = dft_evac_V[(c, g2)]
                            if sem != "V":
                                sems[sem] = max(sems.get(sem, 0), cnt2)
                        for sem, cnt2 in sems.items():
                            K.w(v, sem, cnt2)
                        v.tensor_tensor(out=t1[:], in0=fb[tc][0][:],
                                        in1=fb[tc][0][:], op=ALU.mult)
                        v.tensor_tensor(out=rec[:], in0=fb[tc][1][:],
                                        in1=fb[tc][1][:], op=ALU.mult)
                        ins = v.tensor_tensor(out=t1[:], in0=t1[:], in1=rec[:],
                                              op=ALU.add)
                        K.inc(ins, "V")
                    VEC(_)

                    def _(s, tc=tc):
                        K.w(s, "V")
                        s.activation(out=t1[:], in_=t1[:], func=AF.Ln,
                                     bias=cst[:, 5:6], scale=cst[:, 3:4])
                        s.activation(out=t1[:], in_=t1[:], func=AF.Exp,
                                     bias=0.0, scale=cst[:, 6:7])
                        s.activation(out=t1[:], in_=t1[:], func=AF.Ln,
                                     bias=cst[:, 1:2], scale=cst[:, 3:4])
                        ins = s.activation(out=t1[:], in_=t1[:], func=AF.Exp,
                                           bias=0.0, scale=cst[:, 2:3])
                        K.inc(ins, "Sc")
                    SCA(_)

                    def _(v, tc=tc):
                        K.w(v, "Sc")
                        v.tensor_tensor(out=fb[tc][0][:], in0=fb[tc][0][:],
                                        in1=t1[:], op=ALU.mult)
                        ins = v.tensor_tensor(out=fb[tc][1][:], in0=fb[tc][1][:],
                                              in1=t1[:], op=ALU.mult)
                        K.inc(ins, "V")
                    VEC(_)

                if c in dbg_chunks:
                    for tc3 in range(2):
                        for comp in range(2):
                            dump(f"norm{c}_{tc3}{comp}", fb[tc3][comp])

                def _(v, c=c):
                    sems = {}
                    for g2 in range(20, 30, 2):
                        sem, cnt2 = dft_evac_V[(c, g2)]
                        if sem != "V":
                            sems[sem] = max(sems.get(sem, 0), cnt2)
                    for sem, cnt2 in sems.items():
                        K.w(v, sem, cnt2)
                    a, b = fb[1][0], fb[1][1]
                    cre, cim = fb[2][0], fb[2][1]
                    v.tensor_tensor(out=t1[:], in0=a[:], in1=cre[:], op=ALU.mult)
                    v.tensor_tensor(out=rec[:], in0=a[:], in1=cim[:], op=ALU.mult)
                    v.tensor_tensor(out=a[:], in0=b[:], in1=cim[:], op=ALU.mult)
                    v.tensor_tensor(out=cim[:], in0=b[:], in1=cre[:], op=ALU.mult)
                    v.tensor_tensor(out=b[:], in0=t1[:], in1=a[:], op=ALU.subtract)
                    ins = v.tensor_tensor(out=a[:], in0=rec[:], in1=cim[:],
                                          op=ALU.add)
                    K.inc(ins, "V")
                VEC(_)
                PRRE, PRIM = fb[1][1], fb[1][0]
                if c in dbg_chunks:
                    dump(f"prodre{c}", PRRE)
                    dump(f"prodim{c}", PRIM)

                def _(v, c=c):
                    K.inc(v.engine_nop(), "V")
                    K.w(v, "V")
                    ins = None
                    for comp, srcb, dst in ((0, PRRE, fb[2][0]), (1, PRIM, fb[2][1])):
                        for fc in range(5):
                            col = comp * 5 + fc
                            ins = v.tensor_tensor_scan(
                                out=dst[:, fc * ASCW:(fc + 1) * ASCW],
                                data0=srcb[:, fc * ASCW:(fc + 1) * ASCW],
                                data1=zbuf[:],
                                initial=carry[:, col:col + 1],
                                op0=ALU.add, op1=ALU.add)
                    K.inc(ins, "V")
                    K.w(v, "V")
                    ins = None
                    for comp, dst in ((0, fb[2][0]), (1, fb[2][1])):
                        for fc in range(5):
                            col = comp * 5 + fc
                            ins = v.tensor_copy(
                                carry[:, col:col + 1],
                                dst[:, fc * ASCW + ASCW - 1: fc * ASCW + ASCW])
                    K.inc(ins, "V")
                    scan_V[c] = K.cnt["V"]
                VEC(_)
                if c in dbg_chunks:
                    dump(f"cumre{c}", fb[2][0])
                    dump(f"cumim{c}", fb[2][1])
                    dump(f"carry{c}", carry, cols=10)

                def _(v, c=c):
                    a, b = fb[2][0], fb[2][1]
                    qr, qi = fb[0][0], fb[0][1]
                    v.tensor_tensor(out=t1[:], in0=a[:], in1=qr[:], op=ALU.mult)
                    v.tensor_tensor(out=rec[:], in0=b[:], in1=qi[:], op=ALU.mult)
                    v.tensor_tensor(out=qr[:], in0=b[:], in1=qr[:], op=ALU.mult)
                    v.tensor_tensor(out=qi[:], in0=a[:], in1=qi[:], op=ALU.mult)
                    v.tensor_tensor(out=t1[:], in0=t1[:], in1=rec[:], op=ALU.add)
                    v.tensor_tensor(out=qr[:], in0=qr[:], in1=qi[:],
                                    op=ALU.subtract)
                    v.tensor_copy(gbb[0][:], t1[:])
                    ins = v.tensor_copy(gbb[1][:], qr[:])
                    K.inc(ins, "V")
                    gcast_V[c] = K.cnt["V"]
                VEC(_)
                if c in dbg_chunks:
                    dump(f"gre{c}", gbb[0])
                    dump(f"gim{c}", gbb[1])

                for m in range(AMS):
                    def _(t, c=c, m=m):
                        K.w(t, "V", gcast_V[c])
                        px = pI if m % 2 == 0 else pI2
                        ins = None
                        for bank in range(2):
                            first = True
                            for comp in range(2):
                                ab = abre if comp == 0 else abim
                                g_ = gbb[comp]
                                for fc in range(5):
                                    fs, fz = FCH[fc]
                                    ins = t.matmul(
                                        px[:, bank * 512:(bank + 1) * 512],
                                        g_[0:fz, fc * ASCW + m * 128: fc * ASCW + m * 128 + 128],
                                        ab[0:fz, fc * 1024 + bank * 512: fc * 1024 + bank * 512 + 512],
                                        start=first, stop=(comp == 1 and fc == 4))
                                    first = False
                        K.inc(ins, "T")
                    TEN(_)

                    def _(v, c=c, m=m):
                        K.w(v, "T")
                        px = pI if m % 2 == 0 else pI2
                        ins = v.scalar_tensor_tensor(
                            out=ybuf[:], in0=xin[c % 2][:, m * 1024:(m + 1) * 1024],
                            scalar=cst[:, 4:5], in1=px[:, 0:1024],
                            op0=ALU.mult, op1=ALU.add, accum_out=lns[:, 0:1])
                        K.inc(ins, "V")
                        att_stt_V[(c, m)] = K.cnt["V"]
                        v.scalar_tensor_tensor(
                            out=sqj[:], in0=ybuf[:], scalar=cst[:, 3:4],
                            in1=ybuf[:], op0=ALU.mult, op1=ALU.mult,
                            accum_out=lns[:, 1:2])
                        v.tensor_scalar(out=lns[:, 2:3], in0=lns[:, 0:1],
                                        scalar1=1.0 / D, scalar2=None, op0=ALU.mult)
                        v.tensor_scalar(out=lns[:, 3:4], in0=lns[:, 1:2],
                                        scalar1=1.0 / D, scalar2=None, op0=ALU.mult)
                        ins = v.tensor_tensor(out=lns[:, 4:5], in0=lns[:, 2:3],
                                              in1=lns[:, 2:3], op=ALU.mult)
                        K.flush(v, ins)
                        ins = v.tensor_tensor(out=lns[:, 5:6], in0=lns[:, 3:4],
                                              in1=lns[:, 4:5], op=ALU.subtract)
                        K.inc(ins, "V")
                    VEC(_)

                    if c in dbg_chunks:
                        dump(f"ybuf{c}_{m}", ybuf)

                    def _(s, m=m):
                        K.w(s, "V")
                        s.activation(out=lns[:, 6:7], in_=lns[:, 5:6],
                                     func=AF.Ln, bias=cst[:, 0:1],
                                     scale=cst[:, 3:4])
                        ins = s.activation(out=lns[:, 7:8], in_=lns[:, 6:7],
                                           func=AF.Exp, bias=0.0,
                                           scale=cst[:, 7:8])
                        K.inc(ins, "Sc")
                    SCA(_)

                    def _(v, c=c, m=m):
                        K.w(v, "Sc")
                        if (c, m) != (0, 0):
                            K.w(v, "d")
                        ins = v.tensor_scalar(out=xlnb[m % 2][:], in0=ybuf[:],
                                              scalar1=lns[:, 2:3],
                                              scalar2=lns[:, 7:8],
                                              op0=ALU.subtract, op1=ALU.mult)
                        K.flush(v, ins)
                        col = c * AMS + m
                        ins = v.tensor_reduce(out=maxacc[:, col:col + 1],
                                              in_=xlnb[m % 2][:], axis=AX.X,
                                              op=ALU.max,
                                              apply_absolute_value=True)
                        K.inc(ins, "V")
                    VEC(_)

                    def _(sync, c=c, m=m):
                        K.w(sync, "V")
                        r0 = c * ASCW + m * 128
                        ins = sync.dma_start(out=sp_xln[r0:r0 + 128, :],
                                             in_=xlnb[m % 2][:])
                        K.inc(ins, "d", 16)
                        if upto == "ln1":
                            ins = sync.dma_start(out=y_out[r0:r0 + 128, :],
                                                 in_=xlnb[m % 2][:])
                            K.inc(ins, "d", 16)
                    SYN(_)
        if upto == "ln1":
            def _(g):
                for s in ("T", "V", "Sc", "d", "a", "L0", "L1", "L2", "L3",
                          "X0", "X1", "A0", "A1"):
                    K.w(g, s)
                g.nop()
            GPS(_)
            return nc

        def emit_ar(quants, derive):
            nq = len(quants)

            def _(v):
                K.w(v, "T")
                K.w(v, "Sc")
                K.w(v, "P")
                v.engine_nop()
                K.flush(v, v.engine_nop())
                ins = None
                for j, (lo, hi) in enumerate(quants):
                    ins = v.tensor_reduce(out=mred[:, j:j + 1],
                                          in_=maxacc[:, lo:hi],
                                          axis=AX.X, op=ALU.max,
                                          apply_absolute_value=True)
                K.flush(v, ins)
                K.inc(v.engine_nop(), "V")
            VEC(_)

            def _(t):
                K.w(t, "V")
                K.inc(t.transpose(p512[0][0:nq, 0:128], mred[:, 0:nq], ident[:]),
                      "T")
            TEN(_)

            def _(v):
                K.w(v, "T")
                ins = v.tensor_copy(mrow[0:nq, :], p512[0][0:nq, 0:128])
                K.flush(v, ins)
                ins = v.tensor_reduce(out=mcol[0:nq, 0:1], in_=mrow[0:nq, :],
                                      axis=AX.X, op=ALU.max)
                K.inc(ins, "V")
            VEC(_)

            def _(t):
                K.w(t, "V")
                K.inc(t.transpose(p512[1][0:1, 0:nq], mcol[0:nq, 0:1], ident[0:nq, 0:nq]),
                      "T")
            TEN(_)

            def _(v):
                K.w(v, "T")
                K.inc(v.tensor_copy(scsend[0:1, 0:nq], p512[1][0:1, 0:nq]), "V")
            VEC(_)

            def _(g):
                for s in ("T", "V", "Sc", "d", "a"):
                    K.w(g, s)
                K.inc(g.dma_start(out=cc_i.ap(), in_=scsend[:]), "g", 16)
                K.w(g, "g")
                ins = g.collective_compute(
                    "AllReduce", ALU.max, replica_groups=[list(range(8))],
                    ins=[cc_i.ap().opt()], outs=[cc_o.ap().opt()])
                K.inc(ins, "C")
                K.w(g, "C")
                K.inc(g.dma_start(out=screcv[:], in_=cc_o.ap()), "g", 16)
                K.w(g, "g")
                K.inc(g.nop(), "P")
            GPS(_)

            def _(t):
                K.w(t, "P")
                ins = t.matmul(p512[0][:, 0:16], ones_r[0:1, 0:128],
                               screcv[0:1, 0:16], start=True, stop=True)
                K.inc(ins, "T")
            TEN(_)

            def _(v):
                K.w(v, "T")
                ins = v.tensor_copy(bc16[:], p512[0][:, 0:16])
                K.flush(v, ins)
                derive(v, lambda i: K.flush(v, i))
                ins = v.memset(maxacc[:], 0.0)
                K.inc(ins, "V")
                K.w(v, "V")
            VEC(_)

        def d0(v, fl):
            fl(v.tensor_scalar(out=scal[:, SX:SX + 1], in0=bc16[:, 0:1],
                               scalar1=1.0 / QMAX, scalar2=1e-12,
                               op0=ALU.mult, op1=ALU.max))
            fl(v.reciprocal(scal[:, RX:RX + 1], scal[:, SX:SX + 1]))
        emit_ar([(0, 16)], d0)

        with ExitStack() as pes:
            psb = lambda n, sh, dt: pes.enter_context(nc.sbuf_tensor(n, sh, dt))
            xqT = psb("xqT", [128, NDT * S], BF16)
            xlf = [psb(f"xlf{i}", [128, 1024], F32) for i in range(2)]
            xqf = psb("xqf", [128, 1024], F32)
            xqb = [psb(f"xqb{i}", [128, 1024], F32) for i in range(2)]
            wr = [psb(f"wr{i}", [128, 1024], F32) for i in range(5)]
            wqf = psb("wqf", [128, 1024], F32)
            wqf2 = psb("wqf2", [128, 1024], F32)
            wqb = [psb(f"wqb{i}", [128, 1024], F32) for i in range(3)]
            wT = [psb(f"wT{i}", [128, NDT * 128], BF16) for i in range(4)]
            wm32 = [psb(f"wm32{i}", [128, 32], F32) for i in range(3)]
            wsd = [psb(f"wsd{i}", [128, 32], F32) for i in range(3)]
            wrc = [psb(f"wrc{i}", [128, 32], F32) for i in range(3)]
            ub = [psb(f"ub{i}", [128, 512], F32) for i in range(4)]

            xq_V = {}
            xq_a = {}
            for tl in range(S // 128):
                def _(sync, tl=tl):
                    if tl >= 2:
                        K.w(sync, "V", xq_V[tl - 2])
                    ins = sync.dma_start(out=xlf[tl % 2][:],
                                         in_=sp_xln[tl * 128:(tl + 1) * 128, :])
                    K.inc(ins, "d", 16)
                SYN(_)

                def _(v, tl=tl):
                    K.w(v, "d")
                    K.w(v, "V")
                    if tl >= 2:
                        K.w(v, "V", xq_a[tl - 2])
                    v.tensor_scalar(out=xqf[:], in0=xlf[tl % 2][:],
                                    scalar1=scal[:, RX:RX + 1], scalar2=MAGIC,
                                    op0=ALU.mult, op1=ALU.add)
                    ins = v.tensor_scalar(out=xqb[tl % 2][:], in0=xqf[:],
                                          scalar1=MAGIC,
                                          scalar2=scal[:, SX:SX + 1],
                                          op0=ALU.subtract, op1=ALU.mult)
                    K.inc(ins, "V")
                    xq_V[tl] = K.cnt["V"]
                VEC(_)

                def _(t, tl=tl):
                    K.w(t, "V", xq_V[tl])
                    if tl >= 2:
                        K.w(t, "V", xq_a[tl - 2])
                    px = pI if tl % 2 == 0 else pI2
                    ins = None
                    for dt in range(NDT):
                        ins = t.transpose(px[:, dt * 128:(dt + 1) * 128],
                                          xqb[tl % 2][:, dt * 128:(dt + 1) * 128],
                                          ident[:])
                    K.inc(ins, "T")
                TEN(_)

                def _(v, tl=tl):
                    K.w(v, "T")
                    px = pI if tl % 2 == 0 else pI2
                    oap = dataclasses.replace(
                        xqT[:, tl * 128: tl * 128 + 128],
                        ap=list(xqT[:, 0:1].ap[:1]) + [[S, NDT], [1, 128]])
                    ins = v.tensor_copy(oap, px[:].rearrange("p (a b) -> p a b", b=128))
                    K.inc(ins, "V")
                    xq_a[tl] = K.cnt["V"]
                VEC(_)
            xqT_a = K.cnt["V"]

            deq_V = {}
            wt_a = {}
            wtmm_T = {}
            wtevac_Sc = {}
            mm_T = {}
            red_V = {}
            evac_Sc = {}
            spill_d = {}
            load_d = {}
            NJ = 2 * NIC

            def P_load(j):
                ic, mat = divmod(j, 2)
                wsrc = wu_in if mat == 0 else wg_in

                def _(sync, j=j, ic=ic, wsrc=wsrc):
                    if j >= 5:
                        K.w(sync, "V", deq_V[j - 5])
                    ins = sync.dma_start(out=wr[j % 5][:],
                                         in_=wsrc[ic * 128:(ic + 1) * 128, :])
                    K.inc(ins, "d", 16)
                    load_d[j] = K.cnt["d"]
                SYN(_)

            def R_ops(v, j):
                yield lambda: v.tensor_reduce(
                    out=wm32[j % 3][:].rearrange("p (b o) -> p b o", o=1),
                    in_=wr[j % 5][:].rearrange("p (b w) -> p b w", w=32),
                    axis=AX.X, op=ALU.max, apply_absolute_value=True)
                yield lambda: v.tensor_scalar(
                    out=wsd[j % 3][:], in0=wm32[j % 3][:],
                    scalar1=1.0 / 7.0, scalar2=1e-12,
                    op0=ALU.mult, op1=ALU.max)
                yield lambda: v.reciprocal(wrc[j % 3][:], wsd[j % 3][:])

            def M_ops(v, j):
                yield lambda: v.tensor_tensor(
                    out=wqf[:].rearrange("p (b w) -> p b w", w=32),
                    in0=wr[j % 5][:].rearrange("p (b w) -> p b w", w=32),
                    in1=bcf(wrc[j % 3][:], 32), op=ALU.mult)
                yield lambda: v.tensor_scalar(
                    out=wqf2[:], in0=wqf[:], scalar1=MAGIC,
                    scalar2=MAGIC, op0=ALU.add, op1=ALU.subtract)
                yield lambda: v.tensor_tensor(
                    out=wqb[j % 3][:].rearrange("p (b w) -> p b w", w=32),
                    in0=wqf2[:].rearrange("p (b w) -> p b w", w=32),
                    in1=bcf(wsd[j % 3][:], 32), op=ALU.mult)

            def P_vec(jr, jm):
                def _(v, jr=jr, jm=jm):
                    if jr is not None:
                        K.w(v, "d", load_d[jr])
                    if jm is not None and jm >= 3:
                        K.w(v, "T", wtmm_T[jm - 3])
                    rs = list(R_ops(v, jr)) if jr is not None else []
                    ms = list(M_ops(v, jm)) if jm is not None else []
                    if rs and ms:
                        for f in (rs[0], ms[0], rs[1], ms[1], rs[2]):
                            f()
                        ins = ms[2]()
                    else:
                        ms[0]()
                        ms[1]()
                        ins = ms[2]()
                    K.inc(ins, "V")
                    if jm is not None:
                        deq_V[jm] = K.cnt["V"]
                VEC(_)

            def P_T(j):
                def _(t, j=j):
                    K.w(t, "V", deq_V[j])
                    if j >= 2:
                        K.w(t, "Sc", wtevac_Sc[j - 2])
                    else:
                        K.w(t, "V", xqT_a)
                    px = pI if j % 2 == 0 else pI2
                    ins = None
                    for dt in range(NDT):
                        ins = t.transpose(px[:, dt * 128:(dt + 1) * 128],
                                          wqb[j % 3][:, dt * 128:(dt + 1) * 128],
                                          ident[:])
                    K.inc(ins, "T")
                    wtmm_T[j] = K.cnt["T"]
                TEN(_)

                def _(s, j=j):
                    K.w(s, "T", wtmm_T[j])
                    if j >= 4:
                        K.w(s, "T", mm_T[(j - 4, NSC - 1)])
                    px = pI if j % 2 == 0 else pI2
                    ins = s.copy(wT[j % 4][:], px[:])
                    K.inc(ins, "Sc")
                    wtevac_Sc[j] = K.cnt["Sc"]
                    wt_a[j] = K.cnt["Sc"]
                SCA(_)

            def emit_main(j):
                ic, mat = divmod(j, 2)
                spdst = sp_up if mat == 0 else sp_gate

                def _(t, j=j):
                    K.w(t, "V", xqT_a)
                    K.w(t, "Sc", wt_a[j])
                    for sc in range(NSC):
                        g = j * NSC + sc
                        if g >= 4:
                            K.w(t, "Sc", evac_Sc[g - 4])
                        pb = p512[g % 4]
                        ins = None
                        for dt in range(NDT):
                            ins = t.matmul(
                                pb[:],
                                wT[j % 4][:, dt * 128:(dt + 1) * 128],
                                xqT[:, dt * S + sc * 512: dt * S + sc * 512 + 512],
                                start=(dt == 0), stop=(dt == NDT - 1))
                        K.inc(ins, "T")
                        mm_T[(j, sc)] = K.cnt["T"]
                TEN(_)

                def _(s, j=j):
                    for sc in range(NSC):
                        g = j * NSC + sc
                        K.w(s, "T", mm_T[(j, sc)])
                        if g >= 4:
                            K.w(s, "d", spill_d[g - 4])
                            K.w(s, "V", red_V[g - 4])
                        pb = p512[g % 4]
                        ins = s.copy(ub[g % 4][:], pb[:])
                        K.inc(ins, "Sc")
                        evac_Sc[g] = K.cnt["Sc"]
                SCA(_)

                def _(v, j=j, ic=ic, mat=mat):
                    for sc in range(NSC):
                        g = j * NSC + sc
                        K.w(v, "Sc", evac_Sc[g])
                        col = 16 + mat * 96 + ic * NSC + sc
                        ins = v.tensor_reduce(out=maxacc[:, col:col + 1],
                                              in_=ub[g % 4][:], axis=AX.X,
                                              op=ALU.max,
                                              apply_absolute_value=True)
                        K.inc(ins, "V")
                        red_V[g] = K.cnt["V"]
                VEC(_)

                def _(sync, j=j, ic=ic, spdst=spdst):
                    for sc in range(NSC):
                        g = j * NSC + sc
                        K.w(sync, "Sc", evac_Sc[g])
                        ins = sync.dma_start(
                            out=spdst[ic * 128:(ic + 1) * 128,
                                      sc * 512:(sc + 1) * 512],
                            in_=ub[g % 4][:])
                        K.inc(ins, "d", 16)
                        spill_d[g] = K.cnt["d"]
                SYN(_)

            for j in range(5):
                P_load(j)

            def _(v):
                K.w(v, "d", load_d[3])
                r0 = list(R_ops(v, 0))
                r1 = list(R_ops(v, 1))
                r2 = list(R_ops(v, 2))
                r3 = list(R_ops(v, 3))
                m0 = list(M_ops(v, 0))
                m1 = list(M_ops(v, 1))
                m2 = list(M_ops(v, 2))
                for f in (r0[0], r1[0], r0[1], r1[1], r0[2], r1[2],
                          r2[0], m0[0], r2[1], m0[1], r2[2]):
                    f()
                K.inc(m0[2](), "V")
                deq_V[0] = K.cnt["V"]
                for f in (r3[0], m1[0], r3[1], m1[1], r3[2]):
                    f()
                K.inc(m1[2](), "V")
                deq_V[1] = K.cnt["V"]
                m2[0]()
                m2[1]()
                K.inc(m2[2](), "V")
                deq_V[2] = K.cnt["V"]
            VEC(_)
            P_T(0)
            P_T(1)

            for k in range(NJ):
                if k + 5 < NJ:
                    P_load(k + 5)
                jr = k + 4 if k + 4 < NJ else None
                jm = k + 3 if k + 3 < NJ else None
                if jm is not None:
                    P_vec(jr, jm)
                if k + 2 < NJ:
                    P_T(k + 2)
                emit_main(k)

        def d1(v, fl):
            v.tensor_scalar(out=scal[:, SU:SU + 1], in0=bc16[:, 0:1],
                            scalar1=1.0 / QMAX, scalar2=1e-12,
                            op0=ALU.mult, op1=ALU.max)
            fl(v.tensor_scalar(out=scal[:, SG:SG + 1], in0=bc16[:, 1:2],
                               scalar1=1.0 / QMAX, scalar2=1e-12,
                               op0=ALU.mult, op1=ALU.max))
            v.reciprocal(scal[:, RU:RU + 1], scal[:, SU:SU + 1])
            fl(v.reciprocal(scal[:, RG:RG + 1], scal[:, SG:SG + 1]))
        emit_ar([(16, 112), (112, 208)], d1)

        with ExitStack() as pes:
            psb = lambda n, sh, dt: pes.enter_context(nc.sbuf_tensor(n, sh, dt))
            gt = [psb(f"gt{i}", [128, 2048], F32) for i in range(2)]
            igb = [psb(f"igb{i}", [128, 2048], F32) for i in range(2)]
            w1 = [psb(f"w1{i}", [128, 2048], F32) for i in range(2)]
            w2 = [psb(f"w2{i}", [128, 2048], F32) for i in range(2)]
            w3 = [psb(f"w3{i}", [128, 2048], F32) for i in range(2)]

            ld_d = {}
            q_V = {}
            sp_d = {}
            sig_Sc = {}
            ts2_V = {}
            u_P = {}
            red_V2 = {}
            def emitA2(ic):
                    def _(sync, ic=ic):
                        if ic >= 2:
                            K.w(sync, "V", q_V[ic - 2])
                        ins = sync.dma_start(out=gt[ic % 2][:],
                                             in_=sp_gate[ic * 128:(ic + 1) * 128, :])
                        K.inc(ins, "d", 16)
                        ld_d[ic] = K.cnt["d"]
                    SYN(_)

                    def _(v, ic=ic):
                        K.w(v, "d", ld_d[ic])
                        if ic == 0:
                            K.w(v, "V")
                        if ic >= 2:
                            K.w(v, "d", sp_d[ic - 2])
                            K.w(v, "P", u_P[ic - 2])
                        v.tensor_scalar(out=w1[ic % 2][:], in0=gt[ic % 2][:],
                                        scalar1=scal[:, RG:RG + 1], scalar2=MAGIC,
                                        op0=ALU.mult, op1=ALU.add)
                        ins = v.tensor_scalar(out=igb[ic % 2][:], in0=w1[ic % 2][:],
                                              scalar1=MAGIC, scalar2=None,
                                              op0=ALU.subtract)
                        K.inc(ins, "V")
                        q_V[ic] = K.cnt["V"]
                    VEC(_)

                    def _(sync, ic=ic):
                        K.w(sync, "V", q_V[ic])
                        ins = sync.dma_start(out=sp_ig[ic * 128:(ic + 1) * 128, :],
                                             in_=igb[ic % 2][:])
                        K.inc(ins, "d", 16)
                        sp_d[ic] = K.cnt["d"]
                    SYN(_)

                    def _(s, ic=ic):
                        K.w(s, "V", q_V[ic])
                        if ic >= 2:
                            K.w(s, "V", ts2_V[ic - 2])
                        ins = s.activation(out=w2[ic % 2][:], in_=igb[ic % 2][:],
                                           func=AF.Sigmoid, bias=0.0,
                                           scale=scal[:, SG:SG + 1])
                        K.inc(ins, "Sc")
                        sig_Sc[ic] = K.cnt["Sc"]
                    SCA(_)

                    def _(v, ic=ic):
                        K.w(v, "Sc", sig_Sc[ic])
                        if ic >= 2:
                            K.w(v, "P", u_P[ic - 2])
                        v.tensor_scalar(out=w3[ic % 2][:], in0=w2[ic % 2][:],
                                        scalar1=65536.0, scalar2=MAGIC,
                                        op0=ALU.mult, op1=ALU.add)
                        ins = v.tensor_scalar(out=w3[ic % 2][:], in0=w3[ic % 2][:],
                                              scalar1=MAGIC, scalar2=65535.0,
                                              op0=ALU.subtract, op1=ALU.min)
                        K.inc(ins, "V")
                        ts2_V[ic] = K.cnt["V"]
                    VEC(_)

            def emitB2(ic):
                    def _(g, ic=ic):
                        K.w(g, "V", ts2_V[ic])
                        ins = g.tensor_tensor(out=w1[ic % 2][:], in0=igb[ic % 2][:],
                                              in1=w3[ic % 2][:], op=ALU.mult)
                        K.inc(ins, "P")
                        u_P[ic] = K.cnt["P"]
                    GPS(_)

                    def _(v, ic=ic):
                        K.w(v, "P", u_P[ic])
                        ins = v.tensor_reduce(out=maxacc[:, 16 + ic:17 + ic],
                                              in_=w1[ic % 2][:], axis=AX.X,
                                              op=ALU.max, apply_absolute_value=True)
                        K.inc(ins, "V")
                        red_V2[ic] = K.cnt["V"]
                    VEC(_)

            for ic in range(NIC):
                emitA2(ic)
                if ic >= 1:
                    emitB2(ic - 1)
            emitB2(NIC - 1)
        def d2(v, fl):
            fl(v.tensor_scalar(out=tmpc[:, 0:1], in0=scal[:, SG:SG + 1],
                               scalar1=1.0 / 65536.0, scalar2=None, op0=ALU.mult))
            fl(v.tensor_tensor(out=tmpc[:, 1:2], in0=bc16[:, 0:1],
                               in1=tmpc[:, 0:1], op=ALU.mult))
            fl(v.tensor_scalar(out=scal[:, S2:S2 + 1], in0=tmpc[:, 1:2],
                               scalar1=1.0 / QMAX, scalar2=1e-12,
                               op0=ALU.mult, op1=ALU.max))
            fl(v.reciprocal(tmpc[:, 2:3], scal[:, S2:S2 + 1]))
            fl(v.tensor_tensor(out=scal[:, CR2:CR2 + 1], in0=tmpc[:, 0:1],
                               in1=tmpc[:, 2:3], op=ALU.mult))
            fl(v.tensor_tensor(out=scal[:, S2U:S2U + 1], in0=scal[:, S2:S2 + 1],
                               in1=scal[:, SU:SU + 1], op=ALU.mult))
        emit_ar([(16, 40)], d2)

        with ExitStack() as pes:
            psb = lambda n, sh, dt: pes.enter_context(nc.sbuf_tensor(n, sh, dt))
            ut = [psb(f"ut{i}", [128, 2048], F32) for i in range(2)]
            igb = [psb(f"ig3{i}", [128, 2048], F32) for i in range(2)]
            i2b = [psb(f"i2b{i}", [128, 2048], F32) for i in range(2)]
            w1 = [psb(f"w1b{i}", [128, 2048], F32) for i in range(2)]
            w2 = [psb(f"w2b{i}", [128, 2048], F32) for i in range(2)]
            w3 = [psb(f"w3b{i}", [128, 2048], F32) for i in range(2)]

            ld_d = {}
            sig_Sc = {}
            qu_V = {}
            u_P = {}
            t2_V = {}
            i2_d = {}
            o_P = {}
            red_V3 = {}
            def emitA3(ic):
                    def _(sync, ic=ic):
                        if ic >= 2:
                            K.w(sync, "V", qu_V[ic - 2])
                        ins = sync.dma_start(out=ut[ic % 2][:],
                                             in_=sp_up[ic * 128:(ic + 1) * 128, :])
                        K.inc(ins, "d", 16)
                        ins = sync.dma_start(out=igb[ic % 2][:],
                                             in_=sp_ig[ic * 128:(ic + 1) * 128, :])
                        K.inc(ins, "d", 16)
                        ld_d[ic] = K.cnt["d"]
                    SYN(_)

                    def _(s, ic=ic):
                        K.w(s, "d", ld_d[ic])
                        if ic == 0:
                            K.w(s, "V")
                        if ic >= 2:
                            K.w(s, "V", red_V3[ic - 2])
                        ins = s.activation(out=w2[ic % 2][:], in_=igb[ic % 2][:],
                                           func=AF.Sigmoid, bias=0.0,
                                           scale=scal[:, SG:SG + 1])
                        K.inc(ins, "Sc")
                        sig_Sc[ic] = K.cnt["Sc"]
                    SCA(_)

                    def _(v, ic=ic):
                        K.w(v, "Sc", sig_Sc[ic])
                        if ic == 0:
                            K.w(v, "V")
                        v.tensor_scalar(out=w1[ic % 2][:], in0=ut[ic % 2][:],
                                        scalar1=scal[:, RU:RU + 1], scalar2=MAGIC,
                                        op0=ALU.mult, op1=ALU.add)
                        v.tensor_scalar(out=w1[ic % 2][:], in0=w1[ic % 2][:],
                                        scalar1=MAGIC,
                                        scalar2=scal[:, SU:SU + 1],
                                        op0=ALU.subtract, op1=ALU.mult)
                        v.tensor_scalar(out=w2[ic % 2][:], in0=w2[ic % 2][:],
                                        scalar1=65536.0, scalar2=MAGIC,
                                        op0=ALU.mult, op1=ALU.add)
                        v.tensor_scalar(out=w2[ic % 2][:], in0=w2[ic % 2][:],
                                        scalar1=MAGIC, scalar2=65535.0,
                                        op0=ALU.subtract, op1=ALU.min)
                        ins = v.tensor_tensor(out=w3[ic % 2][:],
                                              in0=igb[ic % 2][:],
                                              in1=w2[ic % 2][:], op=ALU.mult)
                        K.inc(ins, "V")
                        qu_V[ic] = K.cnt["V"]
                    VEC(_)

            def emitB3(ic):
                    def _(v, ic=ic):
                        if ic >= 2:
                            K.w(v, "d", i2_d[ic - 2])
                        v.tensor_scalar(out=w3[ic % 2][:], in0=w3[ic % 2][:],
                                        scalar1=scal[:, CR2:CR2 + 1], scalar2=MAGIC,
                                        op0=ALU.mult, op1=ALU.add)
                        v.tensor_scalar(out=i2b[ic % 2][:], in0=w3[ic % 2][:],
                                        scalar1=MAGIC, scalar2=None,
                                        op0=ALU.subtract)
                        ins = v.tensor_scalar(out=w2[ic % 2][:], in0=w3[ic % 2][:],
                                              scalar1=MAGIC,
                                              scalar2=scal[:, S2:S2 + 1],
                                              op0=ALU.subtract, op1=ALU.mult)
                        K.inc(ins, "V")
                        t2_V[ic] = K.cnt["V"]
                        v.tensor_tensor(out=w3[ic % 2][:], in0=w1[ic % 2][:],
                                        in1=w2[ic % 2][:], op=ALU.mult)
                        ins = v.tensor_reduce(out=maxacc[:, 16 + ic:17 + ic],
                                              in_=w3[ic % 2][:], axis=AX.X,
                                              op=ALU.max, apply_absolute_value=True)
                        K.inc(ins, "V")
                        red_V3[ic] = K.cnt["V"]
                    VEC(_)

                    def _(sync, ic=ic):
                        K.w(sync, "V", t2_V[ic])
                        ins = sync.dma_start(out=sp_i2[ic * 128:(ic + 1) * 128, :],
                                             in_=i2b[ic % 2][:])
                        K.inc(ins, "d", 16)
                        i2_d[ic] = K.cnt["d"]
                    SYN(_)

            for ic in range(NIC):
                emitA3(ic)
                if ic >= 1:
                    emitB3(ic - 1)
            emitB3(NIC - 1)
        def d3(v, fl):
            fl(v.tensor_scalar(out=scal[:, SO:SO + 1], in0=bc16[:, 0:1],
                               scalar1=1.0 / QMAX, scalar2=1e-12,
                               op0=ALU.mult, op1=ALU.max))
            fl(v.reciprocal(scal[:, RO:RO + 1], scal[:, SO:SO + 1]))
        emit_ar([(16, 40)], d3)

        with ExitStack() as pes:
            psb = lambda n, sh, dt: pes.enter_context(nc.sbuf_tensor(n, sh, dt))
            wdT = psb("wdT", [128, NIC * 1024], BF16)
            oqb = [psb(f"oq{i}", [128, NIC * 512], BF16) for i in range(2)]
            wdr = [psb(f"wdr{i}", [128, 1024], F32) for i in range(3)]
            wdf = psb("wdf", [128, 1024], F32)
            wdf2 = psb("wdf2", [128, 1024], F32)
            wdb = [psb(f"wdb{i}", [128, 1024], F32) for i in range(3)]
            wm32 = psb("wm32b", [128, 32], F32)
            wsd = psb("wsdb", [128, 32], F32)
            wrc = psb("wrcb", [128, 32], F32)
            ub = [psb(f"ub4{i}", [128, 512], F32) for i in range(2)]
            i2s = [psb(f"i2s{i}", [128, 512], F32) for i in range(2)]
            v1 = [psb(f"v1{i}", [128, 512], F32) for i in range(2)]
            v2 = [psb(f"v2{i}", [128, 512], F32) for i in range(2)]
            xlf = [psb(f"xlf4{i}", [128, 1024], F32) for i in range(2)]
            ob = [psb(f"ob{i}", [128, 1024], F32) for i in range(2)]

            wld_d = {}
            wdeq_V = {}
            wdt_a = {}
            wdtmm_T = {}
            wdtevac_Sc = {}
            px_sel = lambda ic: (pI if ic % 2 == 0 else pI2)[:]
            for ic in range(NIC):
                def _(sync, ic=ic):
                    if ic >= 3:
                        K.w(sync, "V", wdeq_V[ic - 3])
                    for dt in range(NDT):
                        ins = sync.dma_start(
                            out=wdr[ic % 3][:, dt * 128:(dt + 1) * 128],
                            in_=wd_in[dt * 128:(dt + 1) * 128,
                                      ic * 128:(ic + 1) * 128])
                        K.inc(ins, "d", 16)
                    wld_d[ic] = K.cnt["d"]
                SYN(_)

                def _(v, ic=ic):
                    K.w(v, "d", wld_d[ic])
                    if ic == 0:
                        K.w(v, "V")
                    if ic >= 3:
                        K.w(v, "Sc", wdt_a[ic - 3])
                    ins = v.tensor_reduce(
                        out=wm32[:].rearrange("p (b o) -> p b o", o=1),
                        in_=wdr[ic % 3][:].rearrange("p (b w) -> p b w", w=32),
                        axis=AX.X, op=ALU.max, apply_absolute_value=True)
                    K.flush(v, ins)
                    ins = v.tensor_scalar(out=wsd[:], in0=wm32[:],
                                          scalar1=1.0 / 7.0, scalar2=1e-12,
                                          op0=ALU.mult, op1=ALU.max)
                    K.flush(v, ins)
                    ins = v.reciprocal(wrc[:], wsd[:])
                    K.flush(v, ins)
                    v.tensor_tensor(
                        out=wdf[:].rearrange("p (b w) -> p b w", w=32),
                        in0=wdr[ic % 3][:].rearrange("p (b w) -> p b w", w=32),
                        in1=bcf(wrc[:], 32), op=ALU.mult)
                    v.tensor_scalar(out=wdf2[:], in0=wdf[:], scalar1=MAGIC,
                                    scalar2=MAGIC, op0=ALU.add, op1=ALU.subtract)
                    ins = v.tensor_tensor(
                        out=wdb[ic % 3][:].rearrange("p (b w) -> p b w", w=32),
                        in0=wdf2[:].rearrange("p (b w) -> p b w", w=32),
                        in1=bcf(wsd[:], 32), op=ALU.mult)
                    K.inc(ins, "V")
                    wdeq_V[ic] = K.cnt["V"]
                VEC(_)

                def _(t, ic=ic):
                    K.w(t, "V", wdeq_V[ic])
                    if ic >= 2:
                        K.w(t, "Sc", wdtevac_Sc.get(ic - 2, 0))
                    px = pI if ic % 2 == 0 else pI2
                    ins = None
                    for dt in range(NDT):
                        ins = t.transpose(px[:, dt * 128:(dt + 1) * 128],
                                          wdb[ic % 3][:, dt * 128:(dt + 1) * 128],
                                          ident[:])
                    K.inc(ins, "T")
                    wdtmm_T[ic] = K.cnt["T"]
                TEN(_)

                def _(s, ic=ic):
                    K.w(s, "T", wdtmm_T[ic])
                    ins = s.copy(wdT[:, ic * 1024:(ic + 1) * 1024], px_sel(ic))
                    K.inc(ins, "Sc")
                    wdtevac_Sc[ic] = K.cnt["Sc"]
                    wdt_a[ic] = K.cnt["Sc"]
                SCA(_)
            wdT_all_a = K.cnt["Sc"]

            oqld_d = {}
            oq_V = {}
            oo_P = {}
            mm4_T = {}
            stt4_V = {}
            xld_d = {}
            idn_Sc = {}
            out_d = {}
            def emitA4(sc, ic):
                gg = sc * NIC + ic
                def _(sync, sc=sc, ic=ic, gg=gg):
                    if gg >= 2:
                        K.w(sync, "V", oq_V[gg - 2])
                    ins = sync.dma_start(
                        out=ub[ic % 2][:],
                        in_=sp_up[ic * 128:(ic + 1) * 128,
                                  sc * 512:(sc + 1) * 512])
                    K.inc(ins, "d", 16)
                    ins = sync.dma_start(
                        out=i2s[ic % 2][:],
                        in_=sp_i2[ic * 128:(ic + 1) * 128,
                                  sc * 512:(sc + 1) * 512])
                    K.inc(ins, "d", 16)
                    oqld_d[gg] = K.cnt["d"]
                SYN(_)

                def _(v, sc=sc, ic=ic, gg=gg):
                    K.w(v, "d", oqld_d[gg])
                    if gg == 0:
                        K.w(v, "V")
                    if sc >= 2:
                        K.w(v, "T", mm4_T[(sc - 2, 3)])
                    v.tensor_scalar(out=v1[ic % 2][:], in0=ub[ic % 2][:],
                                    scalar1=scal[:, RU:RU + 1], scalar2=MAGIC,
                                    op0=ALU.mult, op1=ALU.add)
                    v.tensor_scalar(out=v1[ic % 2][:], in0=v1[ic % 2][:],
                                    scalar1=MAGIC,
                                    scalar2=scal[:, SU:SU + 1],
                                    op0=ALU.subtract, op1=ALU.mult)
                    ins = v.tensor_scalar(out=v2[ic % 2][:], in0=i2s[ic % 2][:],
                                          scalar1=scal[:, S2:S2 + 1],
                                          scalar2=None, op0=ALU.mult)
                    K.inc(ins, "V")
                    oq_V[gg] = K.cnt["V"]
                VEC(_)

            def emitB4(sc, ic):
                gg = sc * NIC + ic
                def _(v, sc=sc, ic=ic, gg=gg):
                    v.tensor_tensor(out=v2[ic % 2][:], in0=v1[ic % 2][:],
                                    in1=v2[ic % 2][:], op=ALU.mult)
                    v.tensor_scalar(out=v2[ic % 2][:], in0=v2[ic % 2][:],
                                    scalar1=scal[:, RO:RO + 1], scalar2=MAGIC,
                                    op0=ALU.mult, op1=ALU.add)
                    ins = v.tensor_scalar(out=oqb[sc % 2][:, ic * 512:(ic + 1) * 512],
                                          in0=v2[ic % 2][:], scalar1=MAGIC,
                                          scalar2=scal[:, SO:SO + 1],
                                          op0=ALU.subtract, op1=ALU.mult)
                    K.inc(ins, "V")
                    oq_V[gg] = K.cnt["V"]
                VEC(_)

            oq_done = {}

            def emit_chains(sc):
                prev = None
                for ic in range(NIC):
                    emitA4(sc, ic)
                    if prev is not None:
                        emitB4(*prev)
                    prev = (sc, ic)
                emitB4(*prev)
                oq_done[sc] = K.cnt["V"]

            def emit_mmln(sc):
                for m in range(4):
                    def _(t, sc=sc, m=m):
                        K.w(t, "V", oq_done[sc])
                        K.w(t, "Sc", wdT_all_a)
                        if (sc * 4 + m) >= 2:
                            pm = sc * 4 + m - 2
                            K.w(t, "V", stt4_V[pm])
                        px = pI if m % 2 == 0 else pI2
                        ins = None
                        for bank in range(2):
                            for ic2 in range(NIC):
                                ins = t.matmul(
                                    px[:, bank * 512:(bank + 1) * 512],
                                    oqb[sc % 2][:, ic2 * 512 + m * 128: ic2 * 512 + m * 128 + 128],
                                    wdT[:, ic2 * 1024 + bank * 512: ic2 * 1024 + bank * 512 + 512],
                                    start=(ic2 == 0), stop=(ic2 == NIC - 1))
                        K.inc(ins, "T")
                        mm4_T[(sc, m)] = K.cnt["T"]
                    TEN(_)

                    def _(sync, sc=sc, m=m):
                        sm = sc * 4 + m
                        if sm >= 2:
                            K.w(sync, "V", stt4_V[sm - 2])
                        r0 = sc * 512 + m * 128
                        ins = sync.dma_start(out=xlf[m % 2][:],
                                             in_=sp_xln[r0:r0 + 128, :])
                        K.inc(ins, "d", 16)
                        xld_d[sm] = K.cnt["d"]
                    SYN(_)

                    def _(v, sc=sc, m=m):
                        sm = sc * 4 + m
                        K.w(v, "T", mm4_T[(sc, m)])
                        K.w(v, "d", xld_d[sm])
                        if sm >= 2:
                            K.w(v, "d", out_d[sm - 2])
                        px = pI if m % 2 == 0 else pI2
                        ins = v.scalar_tensor_tensor(
                            out=ob[m % 2][:], in0=px[:, 0:1024],
                            scalar=cst[:, 3:4], in1=xlf[m % 2][:],
                            op0=ALU.mult, op1=ALU.add, accum_out=lns[:, 0:1])
                        K.inc(ins, "V")
                        stt4_V[sm] = K.cnt["V"]
                        v.scalar_tensor_tensor(
                            out=wdf[:], in0=ob[m % 2][:], scalar=cst[:, 3:4],
                            in1=ob[m % 2][:], op0=ALU.mult, op1=ALU.mult,
                            accum_out=lns[:, 1:2])
                        v.tensor_scalar(out=lns[:, 2:3], in0=lns[:, 0:1],
                                        scalar1=1.0 / D, scalar2=None, op0=ALU.mult)
                        v.tensor_scalar(out=lns[:, 3:4], in0=lns[:, 1:2],
                                        scalar1=1.0 / D, scalar2=None, op0=ALU.mult)
                        ins = v.tensor_tensor(out=lns[:, 4:5], in0=lns[:, 2:3],
                                              in1=lns[:, 2:3], op=ALU.mult)
                        K.flush(v, ins)
                        ins = v.tensor_tensor(out=lns[:, 5:6], in0=lns[:, 3:4],
                                              in1=lns[:, 4:5], op=ALU.subtract)
                        K.inc(ins, "V")
                    VEC(_)

                    if c in dbg_chunks:
                        dump(f"ybuf{c}_{m}", ybuf)

                    def _(s, m=m):
                        K.w(s, "V")
                        s.activation(out=lns[:, 6:7], in_=lns[:, 5:6],
                                     func=AF.Ln, bias=cst[:, 0:1],
                                     scale=cst[:, 3:4])
                        ins = s.activation(out=lns[:, 7:8], in_=lns[:, 6:7],
                                           func=AF.Exp, bias=0.0,
                                           scale=cst[:, 7:8])
                        K.inc(ins, "Sc")
                    SCA(_)

                    def _(v, sc=sc, m=m):
                        sm = sc * 4 + m
                        K.w(v, "Sc")
                        ins = v.tensor_scalar(out=ob[m % 2][:], in0=ob[m % 2][:],
                                              scalar1=lns[:, 2:3],
                                              scalar2=lns[:, 7:8],
                                              op0=ALU.subtract, op1=ALU.mult)
                        K.inc(ins, "V")
                        idn_Sc[sm] = ("V", K.cnt["V"])
                    VEC(_)

                    def _(sync, sc=sc, m=m):
                        sm = sc * 4 + m
                        sem, cnt2 = idn_Sc[sm]
                        K.w(sync, sem, cnt2)
                        r0 = sc * 512 + m * 128
                        ins = sync.dma_start(out=y_out[r0:r0 + 128, :],
                                             in_=ob[m % 2][:])
                        K.inc(ins, "d", 16)
                        out_d[sm] = K.cnt["d"]
                    SYN(_)

            for sc in range(NSC):
                emit_chains(sc)
                if sc >= 1:
                    emit_mmln(sc - 1)
            emit_mmln(NSC - 1)

        def _(g):
            for s in ("T", "V", "Sc", "d", "a", "g", "L0", "L1", "L2", "L3",
                      "X0", "X1", "A0", "A1"):
                K.w(g, s)
            g.nop()
        GPS(_)

    return nc


_TABLES = None


def kernel(L0_attn_x, L0_attn_Wq, L0_attn_Wk, L0_attn_Wv,
           L0_ffn_w_gate, L0_ffn_w_up, L0_ffn_w_down):
    global _TABLES
    if _TABLES is None:
        _TABLES = host_tables()
    cm, sm, am, bm, ident = _TABLES
    x = np.ascontiguousarray(np.asarray(L0_attn_x, dtype=np.float32))
    wq = np.ascontiguousarray(np.asarray(L0_attn_Wq, dtype=np.float32))
    wk = np.ascontiguousarray(np.asarray(L0_attn_Wk, dtype=np.float32))
    wv = np.ascontiguousarray(np.asarray(L0_attn_Wv, dtype=np.float32))
    wg = np.ascontiguousarray(np.asarray(L0_ffn_w_gate, dtype=np.float32))
    wu = np.ascontiguousarray(np.asarray(L0_ffn_w_up, dtype=np.float32))
    wd = np.ascontiguousarray(np.asarray(L0_ffn_w_down, dtype=np.float32))

    nc = build_program()
    in_maps = []
    for c in range(8):
        in_maps.append({
            "x_in": x[c], "wq_in": wq, "wk_in": wk, "wv_in": wv,
            "wg_in": wg, "wu_in": wu, "wd_in": wd,
            "t_cm": cm, "t_sm": sm, "t_am": am, "t_bm": bm, "t_id": ident,
        })
    res = run_bass_kernel_spmd(nc, in_maps, core_ids=list(range(8)))
    out = np.stack([res.results[c]["y_out"] for c in range(8)], axis=0)
    return np.ascontiguousarray(out.astype(np.float32))
